# version 19
# baseline (speedup 1.0000x reference)
import numpy as np
from contextlib import ExitStack

import concourse.bass as bass
import concourse.mybir as mybir
from concourse.bass_utils import run_bass_kernel_spmd

F32 = mybir.dt.float32
BF16 = mybir.dt.bfloat16
AF = mybir.ActivationFunctionType
ALU = mybir.AluOpType
AX = mybir.AxisListType.X

DM = 4096
SEQ = 2048
OWN = 1024
RSC = 0.08838834764831843

ENGS = ("sync", "gpsimd", "vector", "scalar", "tensor")
_PHASES = 3
_BTC = 16
_BK = True
_BV = True


class Ctr:
    __slots__ = ("v",)

    def __init__(self):
        self.v = 0

    def inc(self, n):
        self.v += n
        return self.v


class Bld:
    def __init__(self):
        self.q = {e: [] for e in ENGS}
        self.ctr = {}
        self.sem = {}

    def c(self, s):
        if s not in self.ctr:
            self.ctr[s] = Ctr()
        return self.ctr[s]

    def waits(self, eng, *tokens):
        m = {}
        for t in tokens:
            if t is None:
                continue
            s, v = t
            if v > 0:
                m[s] = max(m.get(s, 0), v)
        sem = self.sem
        for s, v in m.items():
            self.q[eng].append(lambda e, ss=s, vv=v: e.wait_ge(sem[ss], vv))

    def dma(self, eng, sname, fn):
        v = self.c(sname).inc(16)
        sem = self.sem
        self.q[eng].append(lambda e, f=fn, s=sname: f(e).then_inc(sem[s], 16))
        return (sname, v)

    def op(self, eng, fn, sname=None):
        if sname is None:
            self.q[eng].append(fn)
            return None
        v = self.c(sname).inc(1)
        sem = self.sem
        self.q[eng].append(lambda e, f=fn, s=sname: f(e).then_inc(sem[s], 1))
        return (sname, v)

    def tok(self, sname):
        return (sname, self.c(sname).v)


def _mk_mm(o, l, r, st, sp):
    return lambda e: e.matmul(o, l, r, start=st, stop=sp)


def _mk_tr(o, i, idn):
    return lambda e: e.transpose(o, i, idn)


def _warm(b, n, o, l, r):
    for _ in range(n):
        b.op("tensor", _mk_mm(o, l, r, True, True), None)


def build_nc():
    nc = bass.Bass("TRN2", target_bir_lowering=False)
    b = Bld()

    xb = nc.dram_tensor("xb", [SEQ, DM], F32, kind="ExternalInput").ap()
    wq = nc.dram_tensor("wq", [DM, DM], F32, kind="ExternalInput").ap()
    wk = nc.dram_tensor("wk", [DM, 1024], F32, kind="ExternalInput").ap()
    wv = nc.dram_tensor("wv", [DM, 1024], F32, kind="ExternalInput").ap()
    wrt = nc.dram_tensor("wrt", [DM, 8], F32, kind="ExternalInput").ap()
    wbt = nc.dram_tensor("wbt", [DM, DM], F32, kind="ExternalInput").ap()
    la = nc.dram_tensor("la", [DM, 128], F32, kind="ExternalInput").ap()
    lb = nc.dram_tensor("lb", [128, DM], F32, kind="ExternalInput").ap()
    bq2 = nc.dram_tensor("bq2", [128, 32], F32, kind="ExternalInput").ap()
    bk2 = nc.dram_tensor("bk2", [128, 8], F32, kind="ExternalInput").ap()
    bb2 = nc.dram_tensor("bb2", [128, 32], F32, kind="ExternalInput").ap()
    bvr = nc.dram_tensor("bvr", [1, 1024], F32, kind="ExternalInput").ap()
    bv2 = nc.dram_tensor("bv2", [128, 8], F32, kind="ExternalInput").ap()
    brr = nc.dram_tensor("brr", [1, 8], F32, kind="ExternalInput").ap()
    cosf = nc.dram_tensor("cosf", [128, SEQ], F32, kind="ExternalInput").ap()
    ssin = nc.dram_tensor("ssin", [128, SEQ], F32, kind="ExternalInput").ap()
    p64 = nc.dram_tensor("p64", [128, 128], F32, kind="ExternalInput").ap()
    i128 = nc.dram_tensor("i128", [128, 128], F32, kind="ExternalInput").ap()
    i4 = nc.dram_tensor("i4", [4, 4], F32, kind="ExternalInput").ap()
    selm = nc.dram_tensor("selm", [128, 16], F32, kind="ExternalInput").ap()
    rep8 = nc.dram_tensor("rep8", [8, 128], F32, kind="ExternalInput").ap()
    onesr = nc.dram_tensor("onesr", [1, 128], F32, kind="ExternalInput").ap()
    onesc = nc.dram_tensor("onesc", [128, 1], F32, kind="ExternalInput").ap()
    epsv = nc.dram_tensor("epsv", [128, 1], F32, kind="ExternalInput").ap()
    s_scr = nc.dram_tensor("s_scr", [DM, OWN], BF16, kind="Internal").ap()
    kx_in = nc.dram_tensor("kx_in", [1024, 1024], BF16, kind="Internal").ap()
    kx_out = nc.dram_tensor("kx_out", [2048, 1024], BF16, kind="Internal").ap()
    vx_in = nc.dram_tensor("vx_in", [1024, 1024], BF16, kind="Internal").ap()
    vx_out = nc.dram_tensor("vx_out", [2048, 1024], BF16, kind="Internal").ap()
    out = nc.dram_tensor("out", [OWN, DM], F32, kind="ExternalOutput").ap()

    es = ExitStack()
    SB = lambda n, sh, d: es.enter_context(nc.sbuf_tensor(n, sh, d))
    PSUM = lambda n, sh, d: es.enter_context(nc.psum_tensor(n, sh, d))
    with es:
        xnf = SB("xnf", [128, 32 * 1024], BF16)
        kf = SB("kf", [128, 8 * 2048], BF16)
        vt = SB("vt", [128, 16 * 1024], BF16)
        wbuf = SB("wbuf", [128, 2 * 4096], BF16)
        arb = SB("arb", [128, 17408], BF16)
        arf = SB("arf", [128, 4096], F32)

        c_cos = SB("c_cos", [128, SEQ], BF16)
        c_ssin = SB("c_ssin", [128, SEQ], BF16)
        c_p64 = SB("c_p64", [128, 128], BF16)
        c_i128 = SB("c_i128", [128, 128], BF16)
        c_i4 = SB("c_i4", [4, 4], BF16)
        c_sel = SB("c_sel", [128, 16], BF16)
        c_rep8 = SB("c_rep8", [8, 128], BF16)
        c_onr = SB("c_onr", [1, 128], BF16)
        c_onc = SB("c_onc", [128, 1], BF16)
        c_eps = SB("c_eps", [128, 1], F32)
        c_wrt = SB("c_wrt", [128, 256], BF16)
        c_bq = SB("c_bq", [128, 32], F32)
        c_bk = SB("c_bk", [128, 8], F32)
        c_bv = SB("c_bv", [128, 8], F32)
        c_bb = SB("c_bb", [128, 32], F32)
        c_bvr = SB("c_bvr", [1, 1024], BF16)
        c_brr = SB("c_brr", [1, 8], BF16)

        st = SB("st", [128, 16], F32)
        nrm = SB("nrm", [1, 32], F32)
        cneg = SB("cneg", [1, 8], BF16)
        cnr = SB("cnr", [128, 4], F32)
        rztok = SB("rztok", [128, 8 * 32], F32)
        mz = SB("mz", [128, 64], F32)
        mw = SB("mw", [128, 64], BF16)
        mtau = SB("mtau", [128, 8], F32)
        msr = SB("msr", [128, 8], F32)
        mcnt = SB("mcnt", [128, 8], F32)
        mrc = SB("mrc", [128, 8], F32)
        mt1 = SB("mt1", [128, 8], F32)
        zero512 = SB("zero512", [128, 8], BF16)
        ones8 = SB("ones8", [128, 8], F32)

        PS1 = PSUM("PS1", [128, 1024], F32)
        PS2 = PSUM("PS2", [128, 1024], F32)
        PS3 = PSUM("PS3", [128, 512], F32)
        PS4 = PSUM("PS4", [4, 512], F32)
        PSB = PSUM("PSB", [128, 2048], BF16)

        SEMS = ["WC", "XB0", "XB1", "WS0", "WS1", "WD0", "WD1", "WD2", "STS0", "STS1",
                "SRL0", "SRL1", "SRL2", "XG0", "KVP", "CC", "KVU", "VP2", "CC2", "VU2",
                "SO0", "SO1", "SO2", "SO3", "SD", "PE", "ACT", "DVE"]
        for s in SEMS:
            b.sem[s] = es.enter_context(nc.semaphore("s" + s))

        block = es.enter_context(nc.Block())

        xnf_v = lambda fc: xnf[:, fc * 1024:(fc + 1) * 1024]
        kf_v = lambda kc: kf[:, kc * 2048:(kc + 1) * 2048]
        vt_v = lambda tc: vt[:, tc * 1024:(tc + 1) * 1024]
        wslot = lambda i: wbuf[:, i * 4096:(i + 1) * 4096]
        xbf_v = lambda r: arb[:, r * 4096:(r + 1) * 4096]
        xnt_v = arb[:, 8192:12288]
        kraw_v = lambda r: arb[:, 12288 + r * 1024: 12288 + (r + 1) * 1024]
        ropeA = arb[:, 14336:14848]
        ropeB = arb[:, 14848:15360]
        qob_v = lambda qc: arb[:, qc * 1024:(qc + 1) * 1024]
        sq_v = arb[:, 4096:6144]
        exps_v = lambda r: arb[:, 6144 + r * 1024: 6144 + (r + 1) * 1024]
        sst_v = lambda r: arb[:, 8192 + r * 512: 8192 + (r + 1) * 512]
        lat_v = lambda fc: arb[:, fc * 128:(fc + 1) * 128]
        lbt_v = arb[:, 4096:8192]
        hf_v = arb[:, 8192:9216]
        hwf_v = arb[:, 9216:10240]
        g_v = lambda r: arb[:, 10240 + r * 512: 10240 + (r + 1) * 512]
        wTt = arb[0:8, 14336:15360]
        rzrow = arb[0:4, 14336:14848]
        sg_v = lambda r: arb[:, 12288 + r * 512: 12288 + (r + 1) * 512]
        srel_v = lambda r: (arb[:, 15360 + r * 1024: 15360 + (r + 1) * 1024]
                            if r < 2 else arb[:, 8192:9216])
        xog_v = arf[:, 0:2048]
        zrow = arf[0:4, 0:512]
        md = arf[:, 512:576]
        mr = arf[:, 576:640]
        mmf = arf[:, 640:704]
        ostg_v = arf[:, 2048:4096]
        xnf3 = xnf[:].rearrange("p (fc c) -> p fc c", c=1024)

        for dst, src in ((c_cos, cosf), (c_ssin, ssin), (c_p64, p64), (c_i128, i128),
                         (c_i4, i4), (c_sel, selm), (c_rep8, rep8), (c_onr, onesr),
                         (c_onc, onesc), (c_eps, epsv), (c_bq, bq2),
                         (c_bk, bk2), (c_bb, bb2), (c_bvr, bvr), (c_brr, brr),
                         (c_bv, bv2)):
            d_, s_ = dst[:], src
            b.dma("gpsimd", "WC", lambda e, d=d_, s=s_: e.dma_start(out=d, in_=s))
        wrt_r = wrt.rearrange("(fc p) c -> p fc c", p=128)
        cwrt3 = c_wrt[:].rearrange("p (fc c) -> p fc c", c=8)
        b.dma("gpsimd", "WC", lambda e, d=cwrt3, s=wrt_r: e.dma_start(out=d, in_=s))
        tCONST = b.tok("WC")
        b.op("vector", lambda e: e.memset(zero512[:], 0.0))
        tz = b.op("vector", lambda e: e.memset(ones8[:], 1.0), "DVE")

        _doB = _PHASES >= 1
        wk_r = wk.rearrange("(fc p) c -> p fc c", p=128)
        wv_r = wv.rearrange("(fc p) c -> p fc c", p=128)
        psb_evac = [None, None]
        txn_by_tc = [None] * 16
        trst_by_tc = [None] * 16
        tevac_by_tc = [None] * 16
        tslot_pe = [None, None]
        txn_half = [None, None]
        tdve_rope_last = None
        tv_ev = None
        tpe_prev_tc = None
        tKgate = [None]
        for half in ((1,) if _doB else ()):
            for tcl in range(8):
                gtc = half * 8 + tcl
                if gtc >= _BTC:
                    continue
                r = gtc % 2
                xsrc = xb[gtc * 128:(gtc + 1) * 128, :]
                xdst = xbf_v(r)
                if gtc >= 2:
                    b.waits("gpsimd", txn_by_tc[gtc - 2])
                tld = b.dma("gpsimd", "XB%d" % r, lambda e, d=xdst, s=xsrc: e.dma_start(out=d, in_=s))
                B0 = (gtc % 2) * 8
                sc = lambda i: st[:, B0 + i: B0 + i + 1]
                t2ago = txn_by_tc[gtc - 2] if gtc >= 2 else None
                r2ago = trst_by_tc[gtc - 2] if gtc >= 2 else None
                b.waits("vector", tld, tCONST, tpe_prev_tc, t2ago)
                b.waits("scalar", tld, tpe_prev_tc, r2ago)
                tsqac = b.op("scalar", lambda e, x=xdst, ac=sc(1): e.activation(
                    xnt_v, x, AF.Square, bias=0.0, scale=1.0, accum_out=ac), "ACT")
                b.op("vector", lambda e, x=xdst, o=sc(0): e.reduce_sum(o, x, axis=AX))
                b.op("vector", lambda e: e.drain())
                b.waits("vector", tsqac)
                b.op("vector", lambda e, o=sc(2), i=sc(0): e.tensor_scalar_mul(o, i, 1.0 / DM))
                b.op("vector", lambda e, o=sc(3), i=sc(1): e.tensor_scalar_mul(o, i, 1.0 / DM))
                b.op("vector", lambda e: e.drain())
                b.op("vector", lambda e, o=sc(6), i=sc(2): e.tensor_mul(o, i, i))
                b.op("vector", lambda e: e.drain())
                tvar = b.op("vector", lambda e, o=sc(3), i=sc(6): e.tensor_sub(o, o, i), "DVE")
                b.waits("scalar", tvar)
                tsd = b.op("scalar", lambda e, o=sc(4), i=sc(3): e.activation(
                    o, i, AF.Sqrt, bias=c_eps[:], scale=1.0), "ACT")
                b.waits("vector", tsd)
                b.op("vector", lambda e, o=sc(5), i=sc(4): e.reciprocal(o, i))
                b.op("vector", lambda e: e.drain())
                b.op("vector", lambda e, o=sc(6), i=sc(2), j=sc(5): e.tensor_mul(o, i, j))
                b.op("vector", lambda e: e.drain())
                trst = b.op("vector", lambda e, o=sc(7), i=sc(6): e.tensor_scalar_mul(o, i, -1.0), "DVE")
                trst_by_tc[gtc] = trst
                b.waits("scalar", trst)
                txn = b.op("scalar", lambda e, x=xdst, bb=sc(7), ss=sc(5): e.activation(
                    xnt_v, x, AF.Identity, bias=bb, scale=ss), "ACT")
                txn_by_tc[gtc] = txn
                for grp in range(8):
                    ph = grp % 2
                    b.waits("tensor", txn, psb_evac[ph])
                    tpe = None
                    for j in range(4):
                        fc = grp * 4 + j
                        o_ = PSB[:, ph * 1024 + j * 128: ph * 1024 + (j + 1) * 128]
                        i_ = xnt_v[:, fc * 128:(fc + 1) * 128]
                        tpe = b.op("tensor", _mk_tr(o_, i_, c_i128[:]),
                                   "PE" if j == 3 else None)
                    b.waits("scalar", tpe)
                    o_ = xnf3[:, grp * 4:(grp + 1) * 4, tcl * 128:(tcl + 1) * 128]
                    i_ = PSB[:, ph * 1024: ph * 1024 + 512]
                    psb_evac[ph] = b.op("scalar", lambda e, oo=o_, ii=i_: e.copy(oo, ii), "ACT")
                    tpe_prev_tc = tpe
                tevac_by_tc[gtc] = psb_evac[ph]
            txn_half[half] = b.tok("ACT")

            for tbw in range(2):
                tgate = tevac_by_tc[half * 8 + 3] if tbw == 0 else txn_half[half]
                for kc in range(8):
                    slot = kc % 2
                    wdst = wslot(slot).rearrange("p (fc c) -> p fc c", c=128)
                    wsrc = wk_r[:, :, kc * 128:(kc + 1) * 128]
                    b.waits("gpsimd", tslot_pe[slot])
                    twk = b.dma("gpsimd", "WS%d" % slot,
                                lambda e, d=wdst, s=wsrc: e.dma_start(out=d, in_=s))
                    b.waits("tensor", twk, tgate, tdve_rope_last)
                    pk = PS1[:, (kc % 2) * 512:(kc % 2) * 512 + 512]
                    tmm = None
                    for fc in range(32):
                        l_ = wslot(slot)[:, fc * 128:(fc + 1) * 128]
                        r_ = xnf_v(fc)[:, tbw * 512:(tbw + 1) * 512]
                        tmm = b.op("tensor", _mk_mm(pk, l_, r_, fc == 0, fc == 31),
                                   "PE" if fc == 31 else None)
                    tslot_pe[slot] = tmm
                    kr = kc % 2
                    kd = kraw_v(kr)[:, 0:512]
                    b.waits("scalar", tmm)
                    tev = b.op("scalar", lambda e, k=kc, d=kd, ii=pk: e.activation(
                        d, ii, AF.Identity, bias=c_bk[:, k:k + 1], scale=1.0), "ACT")
                    b.waits("tensor", tev)
                    psw = PS2[:, (kc % 2) * 512:(kc % 2) * 512 + 512]
                    tsw = b.op("tensor", _mk_mm(psw, c_p64[:], kd, True, True), "PE")
                    b.waits("vector", tsw)
                    co = half * 1024 + tbw * 512
                    c_ = c_cos[:, co:co + 512]
                    s_ = c_ssin[:, co:co + 512]
                    d_ = kf_v(kc)[:, co:co + 512]
                    b.op("vector", lambda e, a=kd, c=c_: e.tensor_mul(ropeA, a, c))
                    b.op("vector", lambda e, p=psw, s=s_: e.tensor_mul(ropeB, p, s))
                    b.op("vector", lambda e: e.drain())
                    tdve_rope_last = b.op("vector", lambda e, d=d_: e.tensor_add(d, ropeA, ropeB), "DVE")
                    b.op("vector", lambda e: e.drain())

            kf3p = kf[:].rearrange("p (kc c) -> p kc c", c=2048)
            b.waits("sync", tdve_rope_last)
            tpk = b.dma("sync", "KVP", lambda e: e.dma_start(
                out=kx_in.rearrange("(t p) c -> p t c", p=128), in_=kf3p[:, :, 1024:2048]))
            b.waits("gpsimd", tpk)
            tKgate[0] = b.op("gpsimd", lambda e: e.collective_compute(
                "AllGather", ALU.bypass,
                replica_groups=[[0, 1], [2, 3], [4, 5], [6, 7]],
                ins=[kx_in], outs=[kx_out]), "CC")

            vt3 = vt[:].rearrange("p (tc c) -> p tc c", c=1024)
            tvtr_ev = [None, None]
            for vc in range(8):
                slot = vc % 2
                psv = PS1 if slot == 0 else PS2
                wdst = wslot(slot).rearrange("p (fc c) -> p fc c", c=128)
                wsrc = wv_r[:, :, vc * 128:(vc + 1) * 128]
                b.waits("gpsimd", tslot_pe[slot])
                twv = b.dma("gpsimd", "WS%d" % slot,
                            lambda e, d=wdst, s=wsrc: e.dma_start(out=d, in_=s))
                b.waits("tensor", twv, txn_half[half], tdve_rope_last, tv_ev)
                tmm = None
                for fc in range(32):
                    l_ = wslot(slot)[:, fc * 128:(fc + 1) * 128]
                    for tb in range(2):
                        o_ = psv[:, tb * 512:(tb + 1) * 512]
                        r_ = xnf_v(fc)[:, tb * 512:(tb + 1) * 512]
                        tmm = b.op("tensor", _mk_mm(o_, l_, r_, fc == 0, fc == 31),
                                   "PE" if (fc == 31 and tb == 1) else None)
                tslot_pe[slot] = tmm
                vr = vc % 2
                b.waits("scalar", tmm)
                tvev = b.op("scalar", lambda e, v=vc, d=kraw_v(vr), ii=psv[:]: e.activation(
                    d, ii, AF.Identity, bias=c_bv[:, v:v + 1], scale=1.0), "ACT")
                ph = vc % 2
                b.waits("tensor", tvev, tvtr_ev[ph])
                ttv = None
                for j in range(8):
                    o_ = PSB[:, ph * 1024 + j * 128: ph * 1024 + (j + 1) * 128]
                    i_ = kraw_v(vr)[:, j * 128:(j + 1) * 128]
                    ttv = b.op("tensor", _mk_tr(o_, i_, c_i128[:]), "PE" if j == 7 else None)
                b.waits("scalar", ttv)
                o_ = vt3[:, half * 8:(half + 1) * 8, vc * 128:(vc + 1) * 128]
                i_ = PSB[:, ph * 1024: ph * 1024 + 1024]
                i3 = i_.rearrange("p (j c) -> p j c", c=128)
                tvtr_ev[ph] = b.op("scalar", lambda e, oo=o_, ii=i3: e.copy(oo, ii), "ACT")
                tv_ev = tvtr_ev[ph]
            b.waits("tensor", tv_ev)
        tB_done = b.tok("ACT")

        kf3 = kf[:].rearrange("p (kc c) -> p kc c", c=2048)
        kxo3 = kx_out.rearrange("(t p) c -> p t c", p=128)
        vxo3 = vx_out.rearrange("(t p) c -> p t c", p=128)
        vt3x = vt[:].rearrange("p (t c) -> p t c", c=1024)
        tcc = tKgate[0]
        b.waits("sync", tcc)
        tu = None
        for h in range(2):
            kdst = kf3[:, :, h * 1024:(h + 1) * 1024]
            tu = b.dma("sync", "KVU", lambda e, d=kdst, s=kxo3[:, h * 8:(h + 1) * 8, :]:
                       e.dma_start(out=d, in_=s))
        tKVU_k = tu
        b.waits("sync", tv_ev)
        tp2 = b.dma("sync", "VP2", lambda e: e.dma_start(
            out=vx_in.rearrange("(t p) c -> p t c", p=128), in_=vt3x[:, 8:16, :]))
        b.waits("gpsimd", tp2)
        tcc2 = b.op("gpsimd", lambda e: e.collective_compute(
            "AllGather", ALU.bypass,
            replica_groups=[[0, 1], [2, 3], [4, 5], [6, 7]],
            ins=[vx_in], outs=[vx_out]), "CC2")
        b.waits("sync", tcc2)
        for h in range(2):
            vdst = vt3x[:, h * 8:(h + 1) * 8, :]
            tu = b.dma("sync", "VU2", lambda e, d=vdst, s=vxo3[:, h * 8:(h + 1) * 8, :]:
                       e.dma_start(out=d, in_=s))
        tKVU = tu

        _doC = _PHASES >= 2
        if _doC:
            _warm(b, 60, PS3[:], c_i128[:], c_cos[:, 0:512])
            b.waits("tensor", tKVU_k)
        wq_r = wq.rearrange("(fc p) c -> p fc c", p=128)
        exp_free = [None, None]
        sst_free = [None, None]
        tqrope_last = tdve_rope_last
        to_ev = None
        trc = None
        trz_prev = None
        ttr_prev = None
        for ob in (range(8) if _doC else ()):
            g = ob
            tqev = [None] * 4
            for wave in range(2):
                psq = [PS1[:, 0:512], PS1[:, 512:1024], PS2[:, 0:512], PS2[:, 512:1024]]
                if wave == 0:
                    b.waits("tensor", tqrope_last, tB_done)
                else:
                    b.waits("tensor", tqev[0], tqev[1])
                tmm = None
                for fcg in range(2):
                    slot = fcg % 2
                    wdst = wslot(slot).rearrange("p (fc c) -> p fc c", c=256)
                    wsrc = wq_r[:, fcg * 16:(fcg + 1) * 16,
                                ob * 512 + wave * 256: ob * 512 + (wave + 1) * 256]
                    b.waits("gpsimd", tslot_pe[slot])
                    twq = b.dma("gpsimd", "WS%d" % slot,
                                lambda e, d=wdst, s=wsrc: e.dma_start(out=d, in_=s))
                    b.waits("tensor", twq)
                    for fcl in range(16):
                        fc = fcg * 16 + fcl
                        for i4l in range(4):
                            qcl = i4l // 2
                            qc = wave * 2 + qcl
                            tb = i4l % 2
                            l_ = wslot(slot)[:, fcl * 256 + qcl * 128: fcl * 256 + (qcl + 1) * 128]
                            r_ = xnf_v(fc)[:, tb * 512:(tb + 1) * 512]
                            tmm = b.op("tensor", _mk_mm(psq[i4l], l_, r_, fc == 0, fc == 31),
                                       "PE" if (fcl == 15 and i4l == 3) else None)
                    tslot_pe[slot] = tmm
                b.waits("scalar", tmm)
                for i2 in range(2):
                    qc = wave * 2 + i2
                    hcol = ob * 4 + qc
                    o_ = qob_v(qc)
                    i_ = (PS1 if i2 == 0 else PS2)[:]
                    tqev[qc] = b.op("scalar", lambda e, oo=o_, ii=i_, h=hcol: e.activation(
                        oo, ii, AF.Identity, bias=c_bq[:, h:h + 1], scale=1.0), "ACT")
            tsw = [None] * 4
            trope_qc = [None] * 4
            b.waits("tensor", tqev[2], tqev[3], tqrope_last)
            for qc in range(4):
                psw = (PS1 if qc % 2 == 0 else PS2)
                if qc >= 2:
                    b.waits("tensor", trope_qc[qc - 2])
                for tb in range(2):
                    o_ = psw[:, tb * 512:(tb + 1) * 512]
                    r_ = qob_v(qc)[:, tb * 512:(tb + 1) * 512]
                    tsw[qc] = b.op("tensor", _mk_mm(o_, c_p64[:], r_, True, True),
                                   "PE" if tb == 1 else None)
                b.waits("vector", tsw[qc])
                rsc4 = [ropeA, ropeB, arb[:, 15360:15872], arb[:, 15872:16384]]
                for tb in range(2):
                    co = 1024 + tb * 512
                    a_ = qob_v(qc)[:, tb * 512:(tb + 1) * 512]
                    c_ = c_cos[:, co:co + 512]
                    s_ = c_ssin[:, co:co + 512]
                    p_ = psw[:, tb * 512:(tb + 1) * 512]
                    b.op("vector", lambda e, a=a_, c=c_, o=rsc4[tb * 2]: e.tensor_mul(o, a, c))
                    b.op("vector", lambda e, p=p_, s=s_, o=rsc4[tb * 2 + 1]: e.tensor_mul(o, p, s))
                b.op("vector", lambda e: e.drain())
                for tb in range(2):
                    a_ = qob_v(qc)[:, tb * 512:(tb + 1) * 512]
                    trope_qc[qc] = b.op("vector", lambda e, a=a_, x=rsc4[tb * 2], y=rsc4[tb * 2 + 1]:
                                        e.tensor_add(a, x, y), "DVE" if tb == 1 else None)
                b.op("vector", lambda e: e.drain())
            tqrope_last = trope_qc[3]
            b.waits("tensor", to_ev)
            _warm(b, 25, PS3[:], c_i128[:], c_cos[:, 0:512])

            sqB = arb[:, 12288:14336]
            b.waits("scalar", tqrope_last)
            tsq01 = b.op("scalar", lambda e: e.activation(
                sq_v, arb[:, 0:2048], AF.Square, bias=0.0, scale=1.0), "ACT")
            tsq23 = b.op("scalar", lambda e: e.activation(
                sqB, arb[:, 2048:4096], AF.Square, bias=0.0, scale=1.0), "ACT")
            b.waits("tensor", tsq01, tqrope_last)
            nslot = [PS1[0:1, 0:512], PS1[0:1, 512:1024], PS2[0:1, 0:512], PS2[0:1, 512:1024]]
            tq01 = None
            for i4 in range(4):
                tq01 = b.op("tensor", _mk_mm(nslot[i4], c_onc[:], sq_v[:, i4 * 512:(i4 + 1) * 512],
                                             True, True), "PE" if i4 == 3 else None)
            b.waits("vector", tq01)
            tqnA = None
            for qc2 in range(2):
                o3 = nrm[:, qc2:qc2 + 1]
                src_ = (PS1 if qc2 == 0 else PS2)[0:1, 0:1024]
                tqnA = b.op("vector", lambda e, oo=o3, ii=src_: e.reduce_max(oo, ii, axis=AX),
                            "DVE" if qc2 == 1 else None)
            b.waits("tensor", tsq23, tqnA)
            tq23 = None
            for i4 in range(4):
                tq23 = b.op("tensor", _mk_mm(nslot[i4], c_onc[:], sqB[:, i4 * 512:(i4 + 1) * 512],
                                             True, True), "PE" if i4 == 3 else None)
            b.waits("scalar", tq01, tKVU)
            tk2 = b.op("scalar", lambda e, g_=g: e.activation(
                sq_v, kf_v(g_), AF.Square, bias=0.0, scale=1.0), "ACT")
            b.waits("vector", tq23)
            tqnB = None
            for qc2 in range(2):
                o3 = nrm[:, 2 + qc2:3 + qc2]
                src_ = (PS1 if qc2 == 0 else PS2)[0:1, 0:1024]
                tqnB = b.op("vector", lambda e, oo=o3, ii=src_: e.reduce_max(oo, ii, axis=AX),
                            "DVE" if qc2 == 1 else None)
            b.waits("tensor", tk2, tqnB)
            tkm = None
            for i4 in range(4):
                tkm = b.op("tensor", _mk_mm(nslot[i4], c_onc[:], sq_v[:, i4 * 512:(i4 + 1) * 512],
                                            True, True), "PE" if i4 == 3 else None)
            b.waits("vector", tkm)
            for qc2 in range(2):
                o3 = nrm[:, 8 + qc2:9 + qc2]
                src_ = (PS1 if qc2 == 0 else PS2)[0:1, 0:1024]
                b.op("vector", lambda e, oo=o3, ii=src_: e.reduce_max(oo, ii, axis=AX), None)
            b.op("vector", lambda e: e.drain())
            b.op("vector", lambda e: e.reduce_max(nrm[:, 12:13], nrm[:, 8:10], axis=AX))
            tdx = b.op("vector", lambda e: e.drain(), "DVE")
            b.waits("scalar", tdx)
            tsq2 = b.op("scalar", lambda e: e.activation(
                nrm[:, 16:20], nrm[:, 0:4], AF.Sqrt, bias=0.0, scale=1.0), "ACT")
            tsq3 = b.op("scalar", lambda e: e.activation(
                nrm[:, 21:22], nrm[:, 12:13], AF.Sqrt, bias=0.0, scale=1.0), "ACT")
            b.waits("vector", tsq3)
            b.op("vector", lambda e: e.tensor_scalar_mul(cneg[:, 0:4], nrm[:, 16:20], -1.02 * RSC))
            b.op("vector", lambda e: e.drain())
            tcn = b.op("vector", lambda e: e.tensor_scalar_mul(
                cneg[:, 0:4], cneg[:, 0:4], nrm[:, 21:22]), "DVE")
            b.waits("tensor", tcn, to_ev)
            tcm = b.op("tensor", _mk_mm(PS3[:, 0:4], c_onr[:], cneg[:, 0:4], True, True), "PE")
            b.waits("scalar", tcm)
            tcr = b.op("scalar", lambda e: e.copy(cnr[:], PS3[:, 0:4]), "ACT")
            b.op("scalar", lambda e: e.drain())

            for qb in range(2):
                for qc in range(4):
                    texp = [None] * 8
                    tZA = [None] * 8
                    zsel = c_sel[:, qc * 4:(qc + 1) * 4]

                    def _emit_S(j, qc=qc, qb=qb):
                        bank = PS1 if (j % 2 == 0) else PS2
                        if j == 0:
                            b.waits("tensor", tcr, trc)
                        if j >= 2:
                            b.waits("tensor", texp[j - 2])
                        tS = None
                        for u in range(2):
                            kcb = j * 2 + u
                            o_ = bank[:, u * 512:(u + 1) * 512]
                            l_ = kf_v(g)[:, kcb * 128:(kcb + 1) * 128]
                            r_ = qob_v(qc)[:, qb * 512:(qb + 1) * 512]
                            tS = b.op("tensor", _mk_mm(o_, l_, r_, True, True),
                                      "PE" if u == 1 else None)
                        return tS

                    def _emit_exp(j, tS, qc=qc):
                        bank = PS1 if (j % 2 == 0) else PS2
                        b.waits("scalar", tS)
                        if exp_free[j % 2] is not None:
                            b.waits("scalar", exp_free[j % 2])
                        o_ = exps_v(j % 2)
                        i_ = bank[:]
                        texp[j] = b.op("scalar", lambda e, oo=o_, ii=i_, q=qc: e.activation(
                            oo, ii, AF.Exp, bias=cnr[:, q:q + 1], scale=RSC), "ACT")

                    def _emit_ZA(j, qc=qc, zsel=zsel):
                        if j == 0:
                            b.waits("tensor", to_ev, trz_prev)
                        b.waits("tensor", texp[j])
                        for u in range(2):
                            eS = exps_v(j % 2)[:, u * 512:(u + 1) * 512]
                            oZ = PS4[0:4, :]
                            stz = (qc == 0 and j == 0 and u == 0)
                            spz = (qc == 3 and j == 7 and u == 1)
                            b.op("tensor", _mk_mm(oZ, zsel, eS, stz, spz), None)
                        for u in range(2):
                            kcb = j * 2 + u
                            eS = exps_v(j % 2)[:, u * 512:(u + 1) * 512]
                            lV = vt_v(kcb)[:, g * 128:(g + 1) * 128]
                            sto = (j == 0 and u == 0)
                            spo = (j == 7 and u == 1)
                            tZA[j] = b.op("tensor", _mk_mm(PS3[:], lV, eS, sto, spo),
                                          "PE" if u == 1 else None)
                        exp_free[j % 2] = tZA[j]

                    for j in range(8):
                        tSj = _emit_S(j)
                        _emit_exp(j, tSj)
                        if j >= 1:
                            _emit_ZA(j - 1)
                    _emit_ZA(7)
                    sslot = (qc * 2 + qb) % 2
                    b.waits("scalar", tZA[7])
                    if sst_free[sslot] is not None:
                        b.waits("scalar", sst_free[sslot])
                    o_ = sst_v(sslot)
                    to_ev = b.op("scalar", lambda e, oo=o_: e.copy(oo, PS3[:]), "ACT")
                    b.waits("sync", to_ev)
                    ssrc = sst_v(sslot)
                    sdst = s_scr[ob * 512 + qc * 128: ob * 512 + (qc + 1) * 128,
                                 qb * 512:(qb + 1) * 512]
                    sst_free[sslot] = b.dma("sync", "STS%d" % sslot,
                                            lambda e, d=sdst, s=ssrc: e.dma_start(out=d, in_=s))
                tza_last = tZA[7]
                b.waits("vector", tza_last, ttr_prev)
                def _recip(e):
                    with nc.allow_low_precision(reason="rz in bf16 feeds TensorE transpose; 0.4% on 1/Z is within the bf16-matmul noise class"):
                        return e.reciprocal(rzrow[:], PS4[0:4, :])
                trz = b.op("vector", _recip, "DVE")
                trz_prev = trz
                b.waits("tensor", trz, trc)
                ttr = None
                for j4 in range(4):
                    o_ = PSB[:, j4 * 4:(j4 + 1) * 4]
                    i2_ = rzrow[:, j4 * 128:(j4 + 1) * 128]
                    ttr = b.op("tensor", _mk_tr(o_, i2_, c_i4[:]), "PE" if j4 == 3 else None)
                ttr_prev = ttr
                b.waits("scalar", ttr)
                for j4 in range(4):
                    tcl = qb * 4 + j4
                    o_ = rztok[:, tcl * 32 + ob * 4: tcl * 32 + ob * 4 + 4]
                    i2_ = PSB[:, j4 * 4:(j4 + 1) * 4]
                    trc = b.op("scalar", lambda e, oo=o_, ii=i2_: e.copy(oo, ii),
                               "ACT" if j4 == 3 else None)
        tC_done_act = b.tok("ACT")
        tC_st0 = b.tok("STS0")
        tC_st1 = b.tok("STS1")
        tC_done_pe = b.tok("PE")
        tC_done_dve = b.tok("DVE")

        if _PHASES >= 3:
            la_r = la.rearrange("(fc p) c -> p fc c", p=128)
            b.waits("gpsimd", tC_done_pe, tC_done_dve, tC_done_act)
            b.dma("gpsimd", "SD", lambda e: e.dma_start(
                out=arb[:, 0:4096].rearrange("p (fc c) -> p fc c", c=128), in_=la_r))
            b.dma("gpsimd", "SD", lambda e: e.dma_start(out=lbt_v, in_=lb))
            tla = tlb = ("SD", 32)

            b.waits("tensor", tC_done_act)
            _warm(b, 30, PS3[:], c_i128[:], c_cos[:, 0:512])
            zr_v = lambda t: mz[:, t * 8:(t + 1) * 8]
            tlog = [None] * 8
            for wave in range(2):
                psq = [PS1[:, 0:512], PS1[:, 512:1024], PS2[:, 0:512], PS2[:, 512:1024]]
                b.waits("tensor", tC_done_act)
                tmm4 = [None] * 4
                for fc in range(32):
                    r_ = c_wrt[:, fc * 8:(fc + 1) * 8]
                    for i4l in range(4):
                        tcl = wave * 4 + i4l
                        l_ = xnf_v(fc)[:, tcl * 128:(tcl + 1) * 128]
                        o_ = psq[i4l][:, 0:8]
                        b.op("tensor", _mk_mm(o_, l_, r_, fc == 0, False), None)
                for i4l in range(4):
                    o_ = psq[i4l][:, 0:8]
                    tmm4[i4l] = b.op("tensor", _mk_mm(o_, c_onr[:], c_brr[:], False, True),
                                     "PE" if i4l == 3 else None)
                b.waits("scalar", tmm4[3])
                for i4l in range(4):
                    tcl = wave * 4 + i4l
                    o_ = zr_v(tcl)
                    i_ = psq[i4l][:, 0:8]
                    tlog[tcl] = b.op("scalar", lambda e, oo=o_, ii=i_: e.copy(oo, ii), "ACT")
                b.waits("tensor", tlog[wave * 4 + 3])

            b.waits("vector", tlog[7], tz)
            NT = 8
            zv = [zr_v(t) for t in range(NT)]
            tauv = [mtau[:, t:t + 1] for t in range(NT)]
            dvv = [md[:, t * 8:(t + 1) * 8] for t in range(NT)]
            rvv = [mr[:, t * 8:(t + 1) * 8] for t in range(NT)]
            mfv = [mmf[:, t * 8:(t + 1) * 8] for t in range(NT)]
            srv = [msr[:, t:t + 1] for t in range(NT)]
            cntv = [mcnt[:, t:t + 1] for t in range(NT)]
            rcv = [mrc[:, t:t + 1] for t in range(NT)]
            t1v = [mt1[:, t:t + 1] for t in range(NT)]
            wv_ = [mw[:, t * 8:(t + 1) * 8] for t in range(NT)]
            z8 = zero512[:]
            DR = lambda: b.op("vector", lambda e: e.drain())
            for t in range(NT):
                b.op("vector", lambda e, a=tauv[t], z=zv[t]: e.reduce_sum(a, z, axis=AX))
            DR()
            for t in range(NT):
                b.op("vector", lambda e, a=tauv[t]: e.tensor_scalar_add(a, a, -1.0))
            DR()
            for t in range(NT):
                b.op("vector", lambda e, a=tauv[t]: e.tensor_scalar_mul(a, a, 0.125))
            for it in range(8):
                DR()
                for t in range(NT):
                    b.op("vector", lambda e, r=rvv[t], z=zv[t], a=tauv[t]: e.scalar_tensor_tensor(
                        out=r, in0=z, scalar=a, in1=z8, op0=ALU.subtract, op1=ALU.max))
                DR()
                for t in range(NT):
                    b.op("vector", lambda e, m=mfv[t], r=rvv[t]: e.scalar_tensor_tensor(
                        out=m, in0=r, scalar=1e30, in1=ones8[:], op0=ALU.mult, op1=ALU.min))
                for t in range(NT):
                    b.op("vector", lambda e, s=srv[t], r=rvv[t]: e.reduce_sum(s, r, axis=AX))
                DR()
                for t in range(NT):
                    b.op("vector", lambda e, c=cntv[t], m=mfv[t]: e.reduce_sum(c, m, axis=AX))
                DR()
                for t in range(NT):
                    b.op("vector", lambda e, rc=rcv[t], c=cntv[t]: e.reciprocal(rc, c))
                for t in range(NT):
                    b.op("vector", lambda e, s=srv[t]: e.tensor_scalar_add(s, s, -1.0))
                DR()
                for t in range(NT):
                    b.op("vector", lambda e, t1=t1v[t], s=srv[t], rc=rcv[t]: e.tensor_mul(t1, s, rc))
                DR()
                for t in range(NT):
                    b.op("vector", lambda e, a=tauv[t], t1=t1v[t]: e.tensor_add(a, a, t1))
            DR()
            for t in range(NT):
                b.op("vector", lambda e, d=dvv[t], z=zv[t], a=tauv[t]: e.tensor_scalar_sub(d, z, a))
            DR()
            tmich = None
            for t in range(NT):
                tmich = b.op("vector", lambda e, w=wv_[t], d=dvv[t]: e.tensor_tensor(w, d, z8, ALU.max),
                             "DVE" if t == NT - 1 else None)

            b.waits("tensor", tla)
            thm = None
            for tb in range(2):
                o_ = PS1[:, tb * 512:(tb + 1) * 512]
                for fc in range(32):
                    l_ = lat_v(fc)
                    r_ = xnf_v(fc)[:, tb * 512:(tb + 1) * 512]
                    thm = b.op("tensor", _mk_mm(o_, l_, r_, fc == 0, fc == 31),
                               "PE" if fc == 31 else None)
            b.waits("scalar", thm, tC_st0, tC_st1)
            thf = b.op("scalar", lambda e: e.copy(hf_v, PS1[:]), "ACT")

            _twT_box = [None]

            def _emit_d2b():
                b.waits("tensor", tmich, tC_done_act)
                twt = None
                for t in range(NT):
                    o_ = PSB[0:8, t * 128:(t + 1) * 128]
                    twt = b.op("tensor", _mk_tr(o_, wv_[t], c_i128[:]), "PE" if t == NT - 1 else None)
                b.waits("scalar", twt)
                _twT_box[0] = b.op("scalar", lambda e: e.copy(wTt[:], PSB[0:8, 0:1024]), "ACT")
            def _emit_wrep():
                b.waits("tensor", _twT_box[0])
                twr = None
                for tb in range(2):
                    o_ = PS2[:, tb * 512:(tb + 1) * 512]
                    r_ = wTt[0:8, tb * 512:(tb + 1) * 512]
                    twr = b.op("tensor", _mk_mm(o_, c_rep8[:], r_, True, True), "PE" if tb == 1 else None)
                b.waits("vector", twr, thf)
                return b.op("vector", lambda e: e.tensor_mul(hwf_v, PS2[:], hf_v), "DVE")
            thw = None

            wbt_r = wbt.rearrange("(fc p) c -> p fc c", p=128)
            wslot3 = lambda i: (wslot(i) if i < 2 else arb[:, 0:4096])
            tD_wslot = [tslot_pe[0], tslot_pe[1], None]
            tsg = [None] * 4
            tsg_guard3 = [None, None, None]
            srel_first2 = True
            tstt_byps = [None] * 4
            tse_last = None
            tst_prev = None
            b.waits("tensor", thf, tlb)
            tD_wslot[2] = thm
            for og in range(16):
                oc0 = og * 2
                tw = [None, None]
                for oci in range(2):
                    oc = oc0 + oci
                    slot = oc % 3
                    b.waits("gpsimd", tD_wslot[slot])
                    wdst = wslot3(slot).rearrange("p (fc c) -> p fc c", c=128)
                    wsrc = wbt_r[:, :, oc * 128:(oc + 1) * 128]
                    tw[oci] = b.dma("gpsimd", "WD%d" % slot, lambda e, d=wdst, s=wsrc: e.dma_start(out=d, in_=s))
                tsr = [None, None]
                for oci in range(2):
                    oc = oc0 + oci
                    srs = oc % 3
                    ssrc = s_scr[oc * 128:(oc + 1) * 128, :]
                    sd = srel_v(srs)
                    if srs == 2 and srel_first2:
                        b.waits("sync", thw)
                    if og == 0:
                        b.waits("sync", b.tok("DVE"))
                    if tsg_guard3[srs] is not None:
                        b.waits("sync", tsg_guard3[srs])
                    b.waits("sync", tC_st0, tC_st1)
                    tsr[oci] = b.dma("sync", "SRL%d" % srs, lambda e, d=sd, s=ssrc: e.dma_start(out=d, in_=s))
                    if srs == 2:
                        srel_first2 = False
                tmm4 = [None] * 4
                if og == 0:
                    b.waits("tensor", tw[0])
                    for fc in range(32):
                        l_ = wslot3(oc0 % 3)[:, fc * 128:(fc + 1) * 128]
                        for tb in range(2):
                            o_ = PS1[:, tb * 512:(tb + 1) * 512]
                            r_ = xnf_v(fc)[:, tb * 512:(tb + 1) * 512]
                            b.op("tensor", _mk_mm(o_, l_, r_, fc == 0, False), None)
                    _emit_d2b()
                    thw = _emit_wrep()
                for oci in range(2):
                    oc = oc0 + oci
                    slot = oc % 3
                    ps = PS1 if oci == 0 else PS2
                    if og == 0 and oci == 1:
                        b.waits("tensor", thw)
                    if not (og == 0 and oci == 0):
                        b.waits("tensor", tw[oci])
                        for fc in range(32):
                            l_ = wslot3(slot)[:, fc * 128:(fc + 1) * 128]
                            for tb in range(2):
                                o_ = ps[:, tb * 512:(tb + 1) * 512]
                                r_ = xnf_v(fc)[:, tb * 512:(tb + 1) * 512]
                                b.op("tensor", _mk_mm(o_, l_, r_, fc == 0, False), None)
                    l_ = lbt_v[:, oc * 128:(oc + 1) * 128]
                    b.waits("tensor", thw)
                    for tb in range(2):
                        o_ = ps[:, tb * 512:(tb + 1) * 512]
                        r_ = hwf_v[:, tb * 512:(tb + 1) * 512]
                        tmm4[oci * 2 + tb] = b.op("tensor", _mk_mm(o_, l_, r_, False, True),
                                                  "PE" if tb == 1 else None)
                tD_wslot[oc0 % 3] = tmm4[1]
                tD_wslot[(oc0 + 1) % 3] = tmm4[3]
                for oci in range(2):
                    oc = oc0 + oci
                    ps = PS1 if oci == 0 else PS2
                    b.waits("scalar", tmm4[oci * 2 + 1])
                    for tb in range(2):
                        gi = oci * 2 + tb
                        o_ = g_v(gi)
                        i_ = ps[:, tb * 512:(tb + 1) * 512]
                        tgv = b.op("scalar", lambda e, oo=o_, ii=i_, c=oc: e.activation(
                            oo, ii, AF.Silu, bias=c_bb[:, c:c + 1], scale=1.0), "ACT")
                        b.waits("vector", tgv, tsr[oci])
                        so_ = sg_v(gi)
                        sr_ = srel_v(oc % 3)[:, tb * 512:(tb + 1) * 512]
                        tsg[gi] = b.op("vector", lambda e, oo=so_, gg=o_, ss=sr_: e.tensor_mul(
                            oo, gg, ss), "DVE")
                tsg_guard3[oc0 % 3] = tsg[1]
                tsg_guard3[(oc0 + 1) % 3] = tsg[3]
                if tse_last is not None:
                    b.waits("sync", tse_last)
                else:
                    b.waits("sync", b.tok("DVE"), b.tok("ACT"))
                xsrc = xb.rearrange("(tc p) c -> p tc c", p=128)[:, 8:16, og * 256:(og + 1) * 256]
                xdst3 = xog_v.rearrange("p (tc c) -> p tc c", c=256)
                txl = b.dma("sync", "XG0", lambda e, d=xdst3, s=xsrc: e.dma_start(out=d, in_=s))
                tse = None
                for tb in range(2):
                    for i4l in range(4):
                        tcl = tb * 4 + i4l
                        qslot = tcl % 2
                        pq = PSB[:, qslot * 1024: qslot * 1024 + 256]
                        if tstt_byps[qslot] is not None:
                            b.waits("tensor", tstt_byps[qslot])
                        ttr = None
                        for oci in range(2):
                            gi = oci * 2 + tb
                            b.waits("tensor", tsg[gi])
                            o_ = pq[:, oci * 128:(oci + 1) * 128]
                            i_ = sg_v(gi)[:, i4l * 128:(i4l + 1) * 128]
                            ttr = b.op("tensor", _mk_tr(o_, i_, c_i128[:]),
                                       "PE" if oci == 1 else None)
                        b.waits("vector", ttr, txl, tst_prev)
                        for oci in range(2):
                            oc = oc0 + oci
                            o_ = ostg_v[:, tcl * 256 + oci * 128: tcl * 256 + (oci + 1) * 128]
                            i0 = pq[:, oci * 128:(oci + 1) * 128]
                            rz_ = rztok[:, tcl * 32 + oc: tcl * 32 + oc + 1]
                            x_ = xog_v[:, tcl * 256 + oci * 128: tcl * 256 + (oci + 1) * 128]
                            tse = b.op("vector", lambda e, oo=o_, ii=i0, rr=rz_, xx=x_:
                                       e.scalar_tensor_tensor(out=oo, in0=ii, scalar=rr, in1=xx,
                                                              op0=ALU.mult, op1=ALU.add),
                                       "DVE" if oci == 1 else None)
                        tstt_byps[qslot] = tse
                tse_last = tse
                b.waits("sync", tse_last)
                odst3 = out.rearrange("(tc p) c -> p tc c", p=128)[:, :, og * 256:(og + 1) * 256]
                osrc3 = ostg_v.rearrange("p (tc c) -> p tc c", c=256)
                tst_prev = b.dma("sync", "SO0",
                                 lambda e, d=odst3, s=osrc3: e.dma_start(out=d, in_=s))
            for qs in range(4):
                b.waits("gpsimd", b.tok("SO%d" % qs))


        for _s in list(b.ctr):
            b.waits("gpsimd", (_s, b.ctr[_s].v))

        @block.sync
        def _(e):
            for fn in b.q["sync"]:
                fn(e)

        @block.gpsimd
        def _(e):
            for fn in b.q["gpsimd"]:
                fn(e)

        @block.vector
        def _(e):
            for fn in b.q["vector"]:
                fn(e)

        @block.scalar
        def _(e):
            for fn in b.q["scalar"]:
                fn(e)

        @block.tensor
        def _(e):
            for fn in b.q["tensor"]:
                fn(e)

    return nc


def _host_consts():
    inv = 10000.0 ** (-(np.arange(0, 128, 2, dtype=np.float64) / 128.0))
    t = np.arange(SEQ, dtype=np.float64)
    ang = inv[:, None] * t[None, :]
    cos = np.cos(ang)
    sin = np.sin(ang)
    cosf = np.concatenate([cos, cos], 0).astype(np.float32)
    ssin = np.concatenate([-sin, sin], 0).astype(np.float32)
    p64 = np.zeros((128, 128), dtype=np.float32)
    for i in range(128):
        p64[(i + 64) % 128, i] = 1.0
    i128 = np.eye(128, dtype=np.float32)
    i4 = np.eye(4, dtype=np.float32)
    selm = np.zeros((128, 16), dtype=np.float32)
    for h in range(4):
        selm[:, h * 4 + h] = 1.0
    rep8 = np.zeros((8, 128), dtype=np.float32)
    for e_ in range(8):
        rep8[e_, e_ * 16:(e_ + 1) * 16] = 1.0
    return dict(cosf=cosf, ssin=ssin, p64=p64, i128=i128, i4=i4, selm=selm,
                rep8=rep8,
                onesr=np.ones((1, 128), dtype=np.float32),
                onesc=np.ones((128, 1), dtype=np.float32),
                epsv=np.full((128, 1), 1e-5, dtype=np.float32))


_NC = None


def kernel(L0_attn_x, L0_attn_wq, L0_attn_bq, L0_attn_wk, L0_attn_bk, L0_attn_wv,
           L0_attn_bv, L0_ffn_W_base, L0_ffn_b_base, L0_ffn_W_router, L0_ffn_b_router,
           L0_ffn_lora_A, L0_ffn_lora_B):
    global _NC
    if _NC is None:
        _NC = build_nc()
    nc = _NC

    x = np.ascontiguousarray(np.asarray(L0_attn_x, dtype=np.float32))
    C = _host_consts()
    shared = {
        "wq": np.ascontiguousarray(np.asarray(L0_attn_wq, np.float32)),
        "wk": np.ascontiguousarray(np.asarray(L0_attn_wk, np.float32)),
        "wv": np.ascontiguousarray(np.asarray(L0_attn_wv, np.float32)),
        "wrt": np.ascontiguousarray(np.asarray(L0_ffn_W_router, np.float32).T),
        "wbt": np.ascontiguousarray(np.asarray(L0_ffn_W_base, np.float32).T),
        "la": np.ascontiguousarray(
            np.asarray(L0_ffn_lora_A, np.float32).transpose(1, 0, 2).reshape(DM, 128)),
        "lb": np.ascontiguousarray(np.asarray(L0_ffn_lora_B, np.float32).reshape(128, DM)),
        "bq2": np.ascontiguousarray(np.asarray(L0_attn_bq, np.float32).reshape(32, 128).T),
        "bk2": np.ascontiguousarray(np.asarray(L0_attn_bk, np.float32).reshape(8, 128).T),
        "bb2": np.ascontiguousarray(np.asarray(L0_ffn_b_base, np.float32).reshape(32, 128).T),
        "bvr": np.ascontiguousarray(np.asarray(L0_attn_bv, np.float32).reshape(1, 1024)),
        "bv2": np.ascontiguousarray(np.asarray(L0_attn_bv, np.float32).reshape(8, 128).T),
        "brr": np.ascontiguousarray(np.asarray(L0_ffn_b_router, np.float32).reshape(1, 8)),
        **C,
    }
    in_maps = []
    for c in range(8):
        bidx = c // 2
        half = c % 2
        order = np.concatenate([np.arange((1 - half) * OWN, (2 - half) * OWN),
                                np.arange(half * OWN, (half + 1) * OWN)])
        m = dict(shared)
        m["xb"] = np.ascontiguousarray(x[bidx][order])
        m["cosf"] = np.ascontiguousarray(C["cosf"][:, order])
        m["ssin"] = np.ascontiguousarray(C["ssin"][:, order])
        in_maps.append(m)
    res = run_bass_kernel_spmd(nc, in_maps, core_ids=list(range(8)))
    outp = np.empty((4, SEQ, DM), dtype=np.float32)
    for c in range(8):
        bidx = c // 2
        half = c % 2
        outp[bidx, half * OWN:(half + 1) * OWN, :] = res.results[c]["out"]
    return outp


# version 20
# speedup vs baseline: 1.0043x; 1.0043x over previous
import numpy as np
from contextlib import ExitStack

import concourse.bass as bass
import concourse.mybir as mybir
from concourse.bass_utils import run_bass_kernel_spmd

F32 = mybir.dt.float32
BF16 = mybir.dt.bfloat16
AF = mybir.ActivationFunctionType
ALU = mybir.AluOpType
AX = mybir.AxisListType.X

DM = 4096
SEQ = 2048
OWN = 1024
RSC = 0.08838834764831843

ENGS = ("sync", "gpsimd", "vector", "scalar", "tensor")
_PHASES = 3
_BTC = 16
_BK = True
_BV = True


class Ctr:
    __slots__ = ("v",)

    def __init__(self):
        self.v = 0

    def inc(self, n):
        self.v += n
        return self.v


class Bld:
    def __init__(self):
        self.q = {e: [] for e in ENGS}
        self.ctr = {}
        self.sem = {}

    def c(self, s):
        if s not in self.ctr:
            self.ctr[s] = Ctr()
        return self.ctr[s]

    def waits(self, eng, *tokens):
        m = {}
        for t in tokens:
            if t is None:
                continue
            s, v = t
            if v > 0:
                m[s] = max(m.get(s, 0), v)
        sem = self.sem
        for s, v in m.items():
            self.q[eng].append(lambda e, ss=s, vv=v: e.wait_ge(sem[ss], vv))

    def dma(self, eng, sname, fn):
        v = self.c(sname).inc(16)
        sem = self.sem
        self.q[eng].append(lambda e, f=fn, s=sname: f(e).then_inc(sem[s], 16))
        return (sname, v)

    def op(self, eng, fn, sname=None):
        if sname is None:
            self.q[eng].append(fn)
            return None
        v = self.c(sname).inc(1)
        sem = self.sem
        self.q[eng].append(lambda e, f=fn, s=sname: f(e).then_inc(sem[s], 1))
        return (sname, v)

    def tok(self, sname):
        return (sname, self.c(sname).v)


def _mk_mm(o, l, r, st, sp):
    return lambda e: e.matmul(o, l, r, start=st, stop=sp)


def _mk_tr(o, i, idn):
    return lambda e: e.transpose(o, i, idn)


def _warm(b, n, o, l, r):
    for _ in range(n):
        b.op("tensor", _mk_mm(o, l, r, True, True), None)


def build_nc():
    nc = bass.Bass("TRN2", target_bir_lowering=False)
    b = Bld()

    xb = nc.dram_tensor("xb", [SEQ, DM], F32, kind="ExternalInput").ap()
    wq = nc.dram_tensor("wq", [DM, DM], F32, kind="ExternalInput").ap()
    wk = nc.dram_tensor("wk", [DM, 1024], F32, kind="ExternalInput").ap()
    wv = nc.dram_tensor("wv", [DM, 1024], F32, kind="ExternalInput").ap()
    wrt = nc.dram_tensor("wrt", [DM, 8], F32, kind="ExternalInput").ap()
    wbt = nc.dram_tensor("wbt", [DM, DM], F32, kind="ExternalInput").ap()
    la = nc.dram_tensor("la", [DM, 128], F32, kind="ExternalInput").ap()
    lb = nc.dram_tensor("lb", [128, DM], F32, kind="ExternalInput").ap()
    bq2 = nc.dram_tensor("bq2", [128, 32], F32, kind="ExternalInput").ap()
    bk2 = nc.dram_tensor("bk2", [128, 8], F32, kind="ExternalInput").ap()
    bb2 = nc.dram_tensor("bb2", [128, 32], F32, kind="ExternalInput").ap()
    bvr = nc.dram_tensor("bvr", [1, 1024], F32, kind="ExternalInput").ap()
    bv2 = nc.dram_tensor("bv2", [128, 8], F32, kind="ExternalInput").ap()
    brr = nc.dram_tensor("brr", [1, 8], F32, kind="ExternalInput").ap()
    cosf = nc.dram_tensor("cosf", [128, SEQ], F32, kind="ExternalInput").ap()
    ssin = nc.dram_tensor("ssin", [128, SEQ], F32, kind="ExternalInput").ap()
    p64 = nc.dram_tensor("p64", [128, 128], F32, kind="ExternalInput").ap()
    i128 = nc.dram_tensor("i128", [128, 128], F32, kind="ExternalInput").ap()
    i4 = nc.dram_tensor("i4", [4, 4], F32, kind="ExternalInput").ap()
    selm = nc.dram_tensor("selm", [128, 16], F32, kind="ExternalInput").ap()
    rep8 = nc.dram_tensor("rep8", [8, 128], F32, kind="ExternalInput").ap()
    onesr = nc.dram_tensor("onesr", [1, 128], F32, kind="ExternalInput").ap()
    onesc = nc.dram_tensor("onesc", [128, 1], F32, kind="ExternalInput").ap()
    epsv = nc.dram_tensor("epsv", [128, 1], F32, kind="ExternalInput").ap()
    s_scr = nc.dram_tensor("s_scr", [DM, OWN], BF16, kind="Internal").ap()
    kx_in = nc.dram_tensor("kx_in", [1024, 1024], BF16, kind="Internal").ap()
    kx_out = nc.dram_tensor("kx_out", [2048, 1024], BF16, kind="Internal").ap()
    vx_in = nc.dram_tensor("vx_in", [1024, 1024], BF16, kind="Internal").ap()
    vx_out = nc.dram_tensor("vx_out", [2048, 1024], BF16, kind="Internal").ap()
    out = nc.dram_tensor("out", [OWN, DM], F32, kind="ExternalOutput").ap()

    es = ExitStack()
    SB = lambda n, sh, d: es.enter_context(nc.sbuf_tensor(n, sh, d))
    PSUM = lambda n, sh, d: es.enter_context(nc.psum_tensor(n, sh, d))
    with es:
        xnf = SB("xnf", [128, 32 * 1024], BF16)
        kf = SB("kf", [128, 8 * 2048], BF16)
        vt = SB("vt", [128, 16 * 1024], BF16)
        wbuf = SB("wbuf", [128, 2 * 4096], BF16)
        arb = SB("arb", [128, 17408], BF16)
        arf = SB("arf", [128, 4096], F32)

        c_cos = SB("c_cos", [128, SEQ], BF16)
        c_ssin = SB("c_ssin", [128, SEQ], BF16)
        c_p64 = SB("c_p64", [128, 128], BF16)
        c_i128 = SB("c_i128", [128, 128], BF16)
        c_i4 = SB("c_i4", [4, 4], BF16)
        c_sel = SB("c_sel", [128, 16], BF16)
        c_rep8 = SB("c_rep8", [8, 128], BF16)
        c_onr = SB("c_onr", [1, 128], BF16)
        c_onc = SB("c_onc", [128, 1], BF16)
        c_eps = SB("c_eps", [128, 1], F32)
        c_wrt = SB("c_wrt", [128, 256], BF16)
        c_bq = SB("c_bq", [128, 32], F32)
        c_bk = SB("c_bk", [128, 8], F32)
        c_bv = SB("c_bv", [128, 8], F32)
        c_bb = SB("c_bb", [128, 32], F32)
        c_bvr = SB("c_bvr", [1, 1024], BF16)
        c_brr = SB("c_brr", [1, 8], BF16)

        st = SB("st", [128, 16], F32)
        nrm = SB("nrm", [1, 32], F32)
        cneg = SB("cneg", [1, 8], BF16)
        cnr = SB("cnr", [128, 4], F32)
        rztok = SB("rztok", [128, 8 * 32], F32)
        mz = SB("mz", [128, 64], F32)
        mw = SB("mw", [128, 64], BF16)
        mtau = SB("mtau", [128, 8], F32)
        msr = SB("msr", [128, 8], F32)
        mcnt = SB("mcnt", [128, 8], F32)
        mrc = SB("mrc", [128, 8], F32)
        mt1 = SB("mt1", [128, 8], F32)
        zero512 = SB("zero512", [128, 8], BF16)
        ones8 = SB("ones8", [128, 8], F32)

        PS1 = PSUM("PS1", [128, 1024], F32)
        PS2 = PSUM("PS2", [128, 1024], F32)
        PS3 = PSUM("PS3", [128, 512], F32)
        PS4 = PSUM("PS4", [4, 512], F32)
        PSB = PSUM("PSB", [128, 2048], BF16)

        SEMS = ["WC", "XB0", "XB1", "WS0", "WS1", "WD0", "WD1", "WD2", "STS0", "STS1",
                "SRL0", "SRL1", "SRL2", "XG0", "KVP", "CC", "KVU", "VP2", "CC2", "VU2",
                "SO0", "SO1", "SO2", "SO3", "SD", "PE", "ACT", "DVE"]
        for s in SEMS:
            b.sem[s] = es.enter_context(nc.semaphore("s" + s))

        block = es.enter_context(nc.Block())

        xnf_v = lambda fc: xnf[:, fc * 1024:(fc + 1) * 1024]
        kf_v = lambda kc: kf[:, kc * 2048:(kc + 1) * 2048]
        vt_v = lambda tc: vt[:, tc * 1024:(tc + 1) * 1024]
        wslot = lambda i: wbuf[:, i * 4096:(i + 1) * 4096]
        xbf_v = lambda r: arb[:, r * 4096:(r + 1) * 4096]
        xnt_v = arb[:, 8192:12288]
        kraw_v = lambda r: arb[:, 12288 + r * 1024: 12288 + (r + 1) * 1024]
        ropeA = arb[:, 14336:14848]
        ropeB = arb[:, 14848:15360]
        qob_v = lambda qc: arb[:, qc * 1024:(qc + 1) * 1024]
        sq_v = arb[:, 4096:6144]
        exps_v = lambda r: arb[:, 6144 + r * 1024: 6144 + (r + 1) * 1024]
        sst_v = lambda r: arb[:, 8192 + r * 512: 8192 + (r + 1) * 512]
        lat_v = lambda fc: arb[:, fc * 128:(fc + 1) * 128]
        lbt_v = arb[:, 4096:8192]
        hf_v = arb[:, 8192:9216]
        hwf_v = arb[:, 9216:10240]
        g_v = lambda r: arb[:, 10240 + r * 512: 10240 + (r + 1) * 512]
        wTt = arb[0:8, 14336:15360]
        rzrow = arb[0:4, 14336:14848]
        sg_v = lambda r: arb[:, 12288 + r * 512: 12288 + (r + 1) * 512]
        srel_v = lambda r: (arb[:, 15360 + r * 1024: 15360 + (r + 1) * 1024]
                            if r < 2 else arb[:, 8192:9216])
        xog_v = arf[:, 0:2048]
        zrow = arf[0:4, 0:512]
        md = arf[:, 512:576]
        mr = arf[:, 576:640]
        mmf = arf[:, 640:704]
        ostg_v = arf[:, 2048:4096]
        xnf3 = xnf[:].rearrange("p (fc c) -> p fc c", c=1024)

        for dst, src in ((c_cos, cosf), (c_ssin, ssin), (c_p64, p64), (c_i128, i128),
                         (c_i4, i4), (c_sel, selm), (c_rep8, rep8), (c_onr, onesr),
                         (c_onc, onesc), (c_eps, epsv), (c_bq, bq2),
                         (c_bk, bk2), (c_bb, bb2), (c_bvr, bvr), (c_brr, brr),
                         (c_bv, bv2)):
            d_, s_ = dst[:], src
            b.dma("gpsimd", "WC", lambda e, d=d_, s=s_: e.dma_start(out=d, in_=s))
        wrt_r = wrt.rearrange("(fc p) c -> p fc c", p=128)
        cwrt3 = c_wrt[:].rearrange("p (fc c) -> p fc c", c=8)
        b.dma("gpsimd", "WC", lambda e, d=cwrt3, s=wrt_r: e.dma_start(out=d, in_=s))
        tCONST = b.tok("WC")
        b.op("vector", lambda e: e.memset(zero512[:], 0.0))
        tz = b.op("vector", lambda e: e.memset(ones8[:], 1.0), "DVE")

        _doB = _PHASES >= 1
        wk_r = wk.rearrange("(fc p) c -> p fc c", p=128)
        wv_r = wv.rearrange("(fc p) c -> p fc c", p=128)
        psb_evac = [None, None]
        txn_by_tc = [None] * 16
        trst_by_tc = [None] * 16
        tevac_by_tc = [None] * 16
        tslot_pe = [None, None]
        txn_half = [None, None]
        tdve_rope_last = None
        tv_ev = None
        tpe_prev_tc = None
        tKgate = [None]
        twk_pre = [None, None]
        for half in ((1,) if _doB else ()):
            for tcl in range(8):
                gtc = half * 8 + tcl
                if gtc >= _BTC:
                    continue
                r = gtc % 2
                xsrc = xb[gtc * 128:(gtc + 1) * 128, :]
                xdst = xbf_v(r)
                if gtc >= 2:
                    b.waits("gpsimd", txn_by_tc[gtc - 2])
                tld = b.dma("gpsimd", "XB%d" % r, lambda e, d=xdst, s=xsrc: e.dma_start(out=d, in_=s))
                if tcl == 1:
                    for kcp in range(2):
                        wdstp = wslot(kcp).rearrange("p (fc c) -> p fc c", c=128)
                        wsrcp = wk_r[:, :, kcp * 128:(kcp + 1) * 128]
                        twk_pre[kcp] = b.dma("gpsimd", "WS%d" % kcp,
                                             lambda e, d=wdstp, s=wsrcp: e.dma_start(out=d, in_=s))
                B0 = (gtc % 2) * 8
                sc = lambda i: st[:, B0 + i: B0 + i + 1]
                t2ago = txn_by_tc[gtc - 2] if gtc >= 2 else None
                r2ago = trst_by_tc[gtc - 2] if gtc >= 2 else None
                b.waits("vector", tld, tCONST, tpe_prev_tc, t2ago)
                b.waits("scalar", tld, tpe_prev_tc, r2ago)
                tsqac = b.op("scalar", lambda e, x=xdst, ac=sc(1): e.activation(
                    xnt_v, x, AF.Square, bias=0.0, scale=1.0, accum_out=ac), "ACT")
                b.op("vector", lambda e, x=xdst, o=sc(0): e.reduce_sum(o, x, axis=AX))
                b.op("vector", lambda e: e.drain())
                b.waits("vector", tsqac)
                b.op("vector", lambda e, o=sc(2), i=sc(0): e.tensor_scalar_mul(o, i, 1.0 / DM))
                b.op("vector", lambda e, o=sc(3), i=sc(1): e.tensor_scalar_mul(o, i, 1.0 / DM))
                b.op("vector", lambda e: e.drain())
                b.op("vector", lambda e, o=sc(6), i=sc(2): e.tensor_mul(o, i, i))
                b.op("vector", lambda e: e.drain())
                tvar = b.op("vector", lambda e, o=sc(3), i=sc(6): e.tensor_sub(o, o, i), "DVE")
                b.waits("scalar", tvar)
                tsd = b.op("scalar", lambda e, o=sc(4), i=sc(3): e.activation(
                    o, i, AF.Sqrt, bias=c_eps[:], scale=1.0), "ACT")
                b.waits("vector", tsd)
                b.op("vector", lambda e, o=sc(5), i=sc(4): e.reciprocal(o, i))
                b.op("vector", lambda e: e.drain())
                b.op("vector", lambda e, o=sc(6), i=sc(2), j=sc(5): e.tensor_mul(o, i, j))
                b.op("vector", lambda e: e.drain())
                trst = b.op("vector", lambda e, o=sc(7), i=sc(6): e.tensor_scalar_mul(o, i, -1.0), "DVE")
                trst_by_tc[gtc] = trst
                b.waits("scalar", trst)
                txn = b.op("scalar", lambda e, x=xdst, bb=sc(7), ss=sc(5): e.activation(
                    xnt_v, x, AF.Identity, bias=bb, scale=ss), "ACT")
                txn_by_tc[gtc] = txn
                for grp in range(8):
                    ph = grp % 2
                    b.waits("tensor", txn, psb_evac[ph])
                    tpe = None
                    for j in range(4):
                        fc = grp * 4 + j
                        o_ = PSB[:, ph * 1024 + j * 128: ph * 1024 + (j + 1) * 128]
                        i_ = xnt_v[:, fc * 128:(fc + 1) * 128]
                        tpe = b.op("tensor", _mk_tr(o_, i_, c_i128[:]),
                                   "PE" if j == 3 else None)
                    b.waits("scalar", tpe)
                    o_ = xnf3[:, grp * 4:(grp + 1) * 4, tcl * 128:(tcl + 1) * 128]
                    i_ = PSB[:, ph * 1024: ph * 1024 + 512]
                    psb_evac[ph] = b.op("scalar", lambda e, oo=o_, ii=i_: e.copy(oo, ii), "ACT")
                    tpe_prev_tc = tpe
                tevac_by_tc[gtc] = psb_evac[ph]
            txn_half[half] = b.tok("ACT")

            for tbw in range(2):
                tgate = tevac_by_tc[half * 8 + 3] if tbw == 0 else txn_half[half]
                for kc in range(8):
                    slot = kc % 2
                    if tbw == 0 and kc < 2:
                        twk = twk_pre[kc]
                    else:
                        wdst = wslot(slot).rearrange("p (fc c) -> p fc c", c=128)
                        wsrc = wk_r[:, :, kc * 128:(kc + 1) * 128]
                        b.waits("gpsimd", tslot_pe[slot])
                        twk = b.dma("gpsimd", "WS%d" % slot,
                                    lambda e, d=wdst, s=wsrc: e.dma_start(out=d, in_=s))
                    b.waits("tensor", twk, tgate, tdve_rope_last)
                    pk = PS1[:, (kc % 2) * 512:(kc % 2) * 512 + 512]
                    tmm = None
                    for fc in range(32):
                        l_ = wslot(slot)[:, fc * 128:(fc + 1) * 128]
                        r_ = xnf_v(fc)[:, tbw * 512:(tbw + 1) * 512]
                        tmm = b.op("tensor", _mk_mm(pk, l_, r_, fc == 0, fc == 31),
                                   "PE" if fc == 31 else None)
                    tslot_pe[slot] = tmm
                    kr = kc % 2
                    kd = kraw_v(kr)[:, 0:512]
                    b.waits("scalar", tmm)
                    tev = b.op("scalar", lambda e, k=kc, d=kd, ii=pk: e.activation(
                        d, ii, AF.Identity, bias=c_bk[:, k:k + 1], scale=1.0), "ACT")
                    b.waits("tensor", tev)
                    psw = PS2[:, (kc % 2) * 512:(kc % 2) * 512 + 512]
                    tsw = b.op("tensor", _mk_mm(psw, c_p64[:], kd, True, True), "PE")
                    b.waits("vector", tsw)
                    co = half * 1024 + tbw * 512
                    c_ = c_cos[:, co:co + 512]
                    s_ = c_ssin[:, co:co + 512]
                    d_ = kf_v(kc)[:, co:co + 512]
                    b.op("vector", lambda e, a=kd, c=c_: e.tensor_mul(ropeA, a, c))
                    b.op("vector", lambda e, p=psw, s=s_: e.tensor_mul(ropeB, p, s))
                    b.op("vector", lambda e: e.drain())
                    tdve_rope_last = b.op("vector", lambda e, d=d_: e.tensor_add(d, ropeA, ropeB), "DVE")
                    b.op("vector", lambda e: e.drain())

            kf3p = kf[:].rearrange("p (kc c) -> p kc c", c=2048)
            b.waits("sync", tdve_rope_last)
            tpk = b.dma("sync", "KVP", lambda e: e.dma_start(
                out=kx_in.rearrange("(t p) c -> p t c", p=128), in_=kf3p[:, :, 1024:2048]))
            b.waits("gpsimd", tpk)
            tKgate[0] = b.op("gpsimd", lambda e: e.collective_compute(
                "AllGather", ALU.bypass,
                replica_groups=[[0, 1], [2, 3], [4, 5], [6, 7]],
                ins=[kx_in], outs=[kx_out]), "CC")

            vt3 = vt[:].rearrange("p (tc c) -> p tc c", c=1024)
            tvtr_ev = [None, None]
            for vc in range(8):
                slot = vc % 2
                psv = PS1 if slot == 0 else PS2
                wdst = wslot(slot).rearrange("p (fc c) -> p fc c", c=128)
                wsrc = wv_r[:, :, vc * 128:(vc + 1) * 128]
                b.waits("gpsimd", tslot_pe[slot])
                twv = b.dma("gpsimd", "WS%d" % slot,
                            lambda e, d=wdst, s=wsrc: e.dma_start(out=d, in_=s))
                b.waits("tensor", twv, txn_half[half], tdve_rope_last, tv_ev)
                tmm = None
                for fc in range(32):
                    l_ = wslot(slot)[:, fc * 128:(fc + 1) * 128]
                    for tb in range(2):
                        o_ = psv[:, tb * 512:(tb + 1) * 512]
                        r_ = xnf_v(fc)[:, tb * 512:(tb + 1) * 512]
                        tmm = b.op("tensor", _mk_mm(o_, l_, r_, fc == 0, fc == 31),
                                   "PE" if (fc == 31 and tb == 1) else None)
                tslot_pe[slot] = tmm
                vr = vc % 2
                b.waits("scalar", tmm)
                tvev = b.op("scalar", lambda e, v=vc, d=kraw_v(vr), ii=psv[:]: e.activation(
                    d, ii, AF.Identity, bias=c_bv[:, v:v + 1], scale=1.0), "ACT")
                ph = vc % 2
                b.waits("tensor", tvev, tvtr_ev[ph])
                ttv = None
                for j in range(8):
                    o_ = PSB[:, ph * 1024 + j * 128: ph * 1024 + (j + 1) * 128]
                    i_ = kraw_v(vr)[:, j * 128:(j + 1) * 128]
                    ttv = b.op("tensor", _mk_tr(o_, i_, c_i128[:]), "PE" if j == 7 else None)
                b.waits("scalar", ttv)
                o_ = vt3[:, half * 8:(half + 1) * 8, vc * 128:(vc + 1) * 128]
                i_ = PSB[:, ph * 1024: ph * 1024 + 1024]
                i3 = i_.rearrange("p (j c) -> p j c", c=128)
                tvtr_ev[ph] = b.op("scalar", lambda e, oo=o_, ii=i3: e.copy(oo, ii), "ACT")
                tv_ev = tvtr_ev[ph]
            b.waits("tensor", tv_ev)
        tB_done = b.tok("ACT")

        kf3 = kf[:].rearrange("p (kc c) -> p kc c", c=2048)
        kxo3 = kx_out.rearrange("(t p) c -> p t c", p=128)
        vxo3 = vx_out.rearrange("(t p) c -> p t c", p=128)
        vt3x = vt[:].rearrange("p (t c) -> p t c", c=1024)
        tcc = tKgate[0]
        b.waits("sync", tcc)
        tu = None
        for h in range(2):
            kdst = kf3[:, :, h * 1024:(h + 1) * 1024]
            tu = b.dma("sync", "KVU", lambda e, d=kdst, s=kxo3[:, h * 8:(h + 1) * 8, :]:
                       e.dma_start(out=d, in_=s))
        tKVU_k = tu
        b.waits("sync", tv_ev)
        tp2 = b.dma("sync", "VP2", lambda e: e.dma_start(
            out=vx_in.rearrange("(t p) c -> p t c", p=128), in_=vt3x[:, 8:16, :]))
        b.waits("gpsimd", tp2)
        tcc2 = b.op("gpsimd", lambda e: e.collective_compute(
            "AllGather", ALU.bypass,
            replica_groups=[[0, 1], [2, 3], [4, 5], [6, 7]],
            ins=[vx_in], outs=[vx_out]), "CC2")
        b.waits("sync", tcc2)
        for h in range(2):
            vdst = vt3x[:, h * 8:(h + 1) * 8, :]
            tu = b.dma("sync", "VU2", lambda e, d=vdst, s=vxo3[:, h * 8:(h + 1) * 8, :]:
                       e.dma_start(out=d, in_=s))
        tKVU = tu
        b.waits("tensor", tKVU_k)

        _doC = _PHASES >= 2
        wq_r = wq.rearrange("(fc p) c -> p fc c", p=128)
        exp_free = [None, None]
        sst_free = [None, None]
        tqrope_last = tdve_rope_last
        to_ev = None
        trc = None
        trz_prev = None
        ttr_prev = None
        for ob in (range(8) if _doC else ()):
            g = ob
            tqev = [None] * 4
            for wave in range(2):
                psq = [PS1[:, 0:512], PS1[:, 512:1024], PS2[:, 0:512], PS2[:, 512:1024]]
                if wave == 0:
                    b.waits("tensor", tqrope_last, tB_done)
                else:
                    b.waits("tensor", tqev[0], tqev[1])
                tmm = None
                for fcg in range(2):
                    slot = fcg % 2
                    wdst = wslot(slot).rearrange("p (fc c) -> p fc c", c=256)
                    wsrc = wq_r[:, fcg * 16:(fcg + 1) * 16,
                                ob * 512 + wave * 256: ob * 512 + (wave + 1) * 256]
                    b.waits("gpsimd", tslot_pe[slot])
                    twq = b.dma("gpsimd", "WS%d" % slot,
                                lambda e, d=wdst, s=wsrc: e.dma_start(out=d, in_=s))
                    b.waits("tensor", twq)
                    for fcl in range(16):
                        fc = fcg * 16 + fcl
                        for i4l in range(4):
                            qcl = i4l // 2
                            qc = wave * 2 + qcl
                            tb = i4l % 2
                            l_ = wslot(slot)[:, fcl * 256 + qcl * 128: fcl * 256 + (qcl + 1) * 128]
                            r_ = xnf_v(fc)[:, tb * 512:(tb + 1) * 512]
                            tmm = b.op("tensor", _mk_mm(psq[i4l], l_, r_, fc == 0, fc == 31),
                                       "PE" if (fcl == 15 and i4l == 3) else None)
                    tslot_pe[slot] = tmm
                b.waits("scalar", tmm)
                for i2 in range(2):
                    qc = wave * 2 + i2
                    hcol = ob * 4 + qc
                    o_ = qob_v(qc)
                    i_ = (PS1 if i2 == 0 else PS2)[:]
                    tqev[qc] = b.op("scalar", lambda e, oo=o_, ii=i_, h=hcol: e.activation(
                        oo, ii, AF.Identity, bias=c_bq[:, h:h + 1], scale=1.0), "ACT")
            tsw = [None] * 4
            trope_qc = [None] * 4
            b.waits("tensor", tqev[2], tqev[3], tqrope_last)
            for qc in range(4):
                psw = (PS1 if qc % 2 == 0 else PS2)
                if qc >= 2:
                    b.waits("tensor", trope_qc[qc - 2])
                for tb in range(2):
                    o_ = psw[:, tb * 512:(tb + 1) * 512]
                    r_ = qob_v(qc)[:, tb * 512:(tb + 1) * 512]
                    tsw[qc] = b.op("tensor", _mk_mm(o_, c_p64[:], r_, True, True),
                                   "PE" if tb == 1 else None)
                b.waits("vector", tsw[qc])
                rsc4 = [ropeA, ropeB, arb[:, 15360:15872], arb[:, 15872:16384]]
                for tb in range(2):
                    co = 1024 + tb * 512
                    a_ = qob_v(qc)[:, tb * 512:(tb + 1) * 512]
                    c_ = c_cos[:, co:co + 512]
                    s_ = c_ssin[:, co:co + 512]
                    p_ = psw[:, tb * 512:(tb + 1) * 512]
                    b.op("vector", lambda e, a=a_, c=c_, o=rsc4[tb * 2]: e.tensor_mul(o, a, c))
                    b.op("vector", lambda e, p=p_, s=s_, o=rsc4[tb * 2 + 1]: e.tensor_mul(o, p, s))
                b.op("vector", lambda e: e.drain())
                for tb in range(2):
                    a_ = qob_v(qc)[:, tb * 512:(tb + 1) * 512]
                    trope_qc[qc] = b.op("vector", lambda e, a=a_, x=rsc4[tb * 2], y=rsc4[tb * 2 + 1]:
                                        e.tensor_add(a, x, y), "DVE" if tb == 1 else None)
                b.op("vector", lambda e: e.drain())
            tqrope_last = trope_qc[3]
            b.waits("tensor", to_ev)
            _warm(b, 25, PS3[:], c_i128[:], c_cos[:, 0:512])

            sqB = arb[:, 12288:14336]
            b.waits("scalar", tqrope_last)
            tsq01 = b.op("scalar", lambda e: e.activation(
                sq_v, arb[:, 0:2048], AF.Square, bias=0.0, scale=1.0), "ACT")
            tsq23 = b.op("scalar", lambda e: e.activation(
                sqB, arb[:, 2048:4096], AF.Square, bias=0.0, scale=1.0), "ACT")
            b.waits("tensor", tsq01, tqrope_last)
            nslot = [PS1[0:1, 0:512], PS1[0:1, 512:1024], PS2[0:1, 0:512], PS2[0:1, 512:1024]]
            tq01 = None
            for i4 in range(4):
                tq01 = b.op("tensor", _mk_mm(nslot[i4], c_onc[:], sq_v[:, i4 * 512:(i4 + 1) * 512],
                                             True, True), "PE" if i4 == 3 else None)
            b.waits("vector", tq01)
            tqnA = None
            for qc2 in range(2):
                o3 = nrm[:, qc2:qc2 + 1]
                src_ = (PS1 if qc2 == 0 else PS2)[0:1, 0:1024]
                tqnA = b.op("vector", lambda e, oo=o3, ii=src_: e.reduce_max(oo, ii, axis=AX),
                            "DVE" if qc2 == 1 else None)
            b.waits("tensor", tsq23, tqnA)
            tq23 = None
            for i4 in range(4):
                tq23 = b.op("tensor", _mk_mm(nslot[i4], c_onc[:], sqB[:, i4 * 512:(i4 + 1) * 512],
                                             True, True), "PE" if i4 == 3 else None)
            b.waits("scalar", tq01, tKVU)
            tk2 = b.op("scalar", lambda e, g_=g: e.activation(
                sq_v, kf_v(g_), AF.Square, bias=0.0, scale=1.0), "ACT")
            b.waits("vector", tq23)
            tqnB = None
            for qc2 in range(2):
                o3 = nrm[:, 2 + qc2:3 + qc2]
                src_ = (PS1 if qc2 == 0 else PS2)[0:1, 0:1024]
                tqnB = b.op("vector", lambda e, oo=o3, ii=src_: e.reduce_max(oo, ii, axis=AX),
                            "DVE" if qc2 == 1 else None)
            b.waits("tensor", tk2, tqnB)
            tkm = None
            for i4 in range(4):
                tkm = b.op("tensor", _mk_mm(nslot[i4], c_onc[:], sq_v[:, i4 * 512:(i4 + 1) * 512],
                                            True, True), "PE" if i4 == 3 else None)
            b.waits("vector", tkm)
            for qc2 in range(2):
                o3 = nrm[:, 8 + qc2:9 + qc2]
                src_ = (PS1 if qc2 == 0 else PS2)[0:1, 0:1024]
                b.op("vector", lambda e, oo=o3, ii=src_: e.reduce_max(oo, ii, axis=AX), None)
            b.op("vector", lambda e: e.drain())
            b.op("vector", lambda e: e.reduce_max(nrm[:, 12:13], nrm[:, 8:10], axis=AX))
            tdx = b.op("vector", lambda e: e.drain(), "DVE")
            b.waits("scalar", tdx)
            tsq2 = b.op("scalar", lambda e: e.activation(
                nrm[:, 16:20], nrm[:, 0:4], AF.Sqrt, bias=0.0, scale=1.0), "ACT")
            tsq3 = b.op("scalar", lambda e: e.activation(
                nrm[:, 21:22], nrm[:, 12:13], AF.Sqrt, bias=0.0, scale=1.0), "ACT")
            b.waits("vector", tsq3)
            b.op("vector", lambda e: e.tensor_scalar_mul(cneg[:, 0:4], nrm[:, 16:20], -1.02 * RSC))
            b.op("vector", lambda e: e.drain())
            tcn = b.op("vector", lambda e: e.tensor_scalar_mul(
                cneg[:, 0:4], cneg[:, 0:4], nrm[:, 21:22]), "DVE")
            b.waits("tensor", tcn, to_ev)
            tcm = b.op("tensor", _mk_mm(PS3[:, 0:4], c_onr[:], cneg[:, 0:4], True, True), "PE")
            b.waits("scalar", tcm)
            tcr = b.op("scalar", lambda e: e.copy(cnr[:], PS3[:, 0:4]), "ACT")
            b.op("scalar", lambda e: e.drain())

            for qb in range(2):
                for qc in range(4):
                    texp = [None] * 8
                    tZA = [None] * 8
                    zsel = c_sel[:, qc * 4:(qc + 1) * 4]

                    def _emit_S(j, qc=qc, qb=qb):
                        bank = PS1 if (j % 2 == 0) else PS2
                        if j == 0:
                            b.waits("tensor", tcr, trc)
                        if j >= 2:
                            b.waits("tensor", texp[j - 2])
                        tS = None
                        for u in range(2):
                            kcb = j * 2 + u
                            o_ = bank[:, u * 512:(u + 1) * 512]
                            l_ = kf_v(g)[:, kcb * 128:(kcb + 1) * 128]
                            r_ = qob_v(qc)[:, qb * 512:(qb + 1) * 512]
                            tS = b.op("tensor", _mk_mm(o_, l_, r_, True, True),
                                      "PE" if u == 1 else None)
                        return tS

                    def _emit_exp(j, tS, qc=qc):
                        bank = PS1 if (j % 2 == 0) else PS2
                        b.waits("scalar", tS)
                        if exp_free[j % 2] is not None:
                            b.waits("scalar", exp_free[j % 2])
                        o_ = exps_v(j % 2)
                        i_ = bank[:]
                        texp[j] = b.op("scalar", lambda e, oo=o_, ii=i_, q=qc: e.activation(
                            oo, ii, AF.Exp, bias=cnr[:, q:q + 1], scale=RSC), "ACT")

                    def _emit_ZA(j, qc=qc, zsel=zsel):
                        if j == 0:
                            b.waits("tensor", to_ev, trz_prev)
                        b.waits("tensor", texp[j])
                        for u in range(2):
                            eS = exps_v(j % 2)[:, u * 512:(u + 1) * 512]
                            oZ = PS4[0:4, :]
                            stz = (qc == 0 and j == 0 and u == 0)
                            spz = (qc == 3 and j == 7 and u == 1)
                            b.op("tensor", _mk_mm(oZ, zsel, eS, stz, spz), None)
                        for u in range(2):
                            kcb = j * 2 + u
                            eS = exps_v(j % 2)[:, u * 512:(u + 1) * 512]
                            lV = vt_v(kcb)[:, g * 128:(g + 1) * 128]
                            sto = (j == 0 and u == 0)
                            spo = (j == 7 and u == 1)
                            tZA[j] = b.op("tensor", _mk_mm(PS3[:], lV, eS, sto, spo),
                                          "PE" if u == 1 else None)
                        exp_free[j % 2] = tZA[j]

                    for j in range(8):
                        tSj = _emit_S(j)
                        _emit_exp(j, tSj)
                        if j >= 1:
                            _emit_ZA(j - 1)
                    _emit_ZA(7)
                    sslot = (qc * 2 + qb) % 2
                    b.waits("scalar", tZA[7])
                    if sst_free[sslot] is not None:
                        b.waits("scalar", sst_free[sslot])
                    o_ = sst_v(sslot)
                    to_ev = b.op("scalar", lambda e, oo=o_: e.copy(oo, PS3[:]), "ACT")
                    b.waits("sync", to_ev)
                    ssrc = sst_v(sslot)
                    sdst = s_scr[ob * 512 + qc * 128: ob * 512 + (qc + 1) * 128,
                                 qb * 512:(qb + 1) * 512]
                    sst_free[sslot] = b.dma("sync", "STS%d" % sslot,
                                            lambda e, d=sdst, s=ssrc: e.dma_start(out=d, in_=s))
                tza_last = tZA[7]
                b.waits("vector", tza_last, ttr_prev)
                def _recip(e):
                    with nc.allow_low_precision(reason="rz in bf16 feeds TensorE transpose; 0.4% on 1/Z is within the bf16-matmul noise class"):
                        return e.reciprocal(rzrow[:], PS4[0:4, :])
                trz = b.op("vector", _recip, "DVE")
                trz_prev = trz
                b.waits("tensor", trz, trc)
                ttr = None
                for j4 in range(4):
                    o_ = PSB[:, j4 * 4:(j4 + 1) * 4]
                    i2_ = rzrow[:, j4 * 128:(j4 + 1) * 128]
                    ttr = b.op("tensor", _mk_tr(o_, i2_, c_i4[:]), "PE" if j4 == 3 else None)
                ttr_prev = ttr
                b.waits("scalar", ttr)
                for j4 in range(4):
                    tcl = qb * 4 + j4
                    o_ = rztok[:, tcl * 32 + ob * 4: tcl * 32 + ob * 4 + 4]
                    i2_ = PSB[:, j4 * 4:(j4 + 1) * 4]
                    trc = b.op("scalar", lambda e, oo=o_, ii=i2_: e.copy(oo, ii),
                               "ACT" if j4 == 3 else None)
        tC_done_act = b.tok("ACT")
        tC_st0 = b.tok("STS0")
        tC_st1 = b.tok("STS1")
        tC_done_pe = b.tok("PE")
        tC_done_dve = b.tok("DVE")

        if _PHASES >= 3:
            la_r = la.rearrange("(fc p) c -> p fc c", p=128)
            b.waits("gpsimd", tC_done_pe, tC_done_dve, tC_done_act)
            b.dma("gpsimd", "SD", lambda e: e.dma_start(
                out=arb[:, 0:4096].rearrange("p (fc c) -> p fc c", c=128), in_=la_r))
            b.dma("gpsimd", "SD", lambda e: e.dma_start(out=lbt_v, in_=lb))
            tla = tlb = ("SD", 32)

            zr_v = lambda t: mz[:, t * 8:(t + 1) * 8]
            tlog = [None] * 8
            for wave in range(2):
                psq = [PS1[:, 0:512], PS1[:, 512:1024], PS2[:, 0:512], PS2[:, 512:1024]]
                b.waits("tensor", tC_done_act)
                tmm4 = [None] * 4
                for fc in range(32):
                    r_ = c_wrt[:, fc * 8:(fc + 1) * 8]
                    for i4l in range(4):
                        tcl = wave * 4 + i4l
                        l_ = xnf_v(fc)[:, tcl * 128:(tcl + 1) * 128]
                        o_ = psq[i4l][:, 0:8]
                        b.op("tensor", _mk_mm(o_, l_, r_, fc == 0, False), None)
                for i4l in range(4):
                    o_ = psq[i4l][:, 0:8]
                    tmm4[i4l] = b.op("tensor", _mk_mm(o_, c_onr[:], c_brr[:], False, True),
                                     "PE" if i4l == 3 else None)
                b.waits("scalar", tmm4[3])
                for i4l in range(4):
                    tcl = wave * 4 + i4l
                    o_ = zr_v(tcl)
                    i_ = psq[i4l][:, 0:8]
                    tlog[tcl] = b.op("scalar", lambda e, oo=o_, ii=i_: e.copy(oo, ii), "ACT")
                b.waits("tensor", tlog[wave * 4 + 3])

            b.waits("vector", tlog[7], tz)
            NT = 8
            zv = [zr_v(t) for t in range(NT)]
            tauv = [mtau[:, t:t + 1] for t in range(NT)]
            dvv = [md[:, t * 8:(t + 1) * 8] for t in range(NT)]
            rvv = [mr[:, t * 8:(t + 1) * 8] for t in range(NT)]
            mfv = [mmf[:, t * 8:(t + 1) * 8] for t in range(NT)]
            srv = [msr[:, t:t + 1] for t in range(NT)]
            cntv = [mcnt[:, t:t + 1] for t in range(NT)]
            rcv = [mrc[:, t:t + 1] for t in range(NT)]
            t1v = [mt1[:, t:t + 1] for t in range(NT)]
            wv_ = [mw[:, t * 8:(t + 1) * 8] for t in range(NT)]
            z8 = zero512[:]
            DR = lambda: b.op("vector", lambda e: e.drain())
            for t in range(NT):
                b.op("vector", lambda e, a=tauv[t], z=zv[t]: e.reduce_sum(a, z, axis=AX))
            DR()
            for t in range(NT):
                b.op("vector", lambda e, a=tauv[t]: e.tensor_scalar_add(a, a, -1.0))
            DR()
            for t in range(NT):
                b.op("vector", lambda e, a=tauv[t]: e.tensor_scalar_mul(a, a, 0.125))
            for it in range(8):
                DR()
                for t in range(NT):
                    b.op("vector", lambda e, r=rvv[t], z=zv[t], a=tauv[t]: e.scalar_tensor_tensor(
                        out=r, in0=z, scalar=a, in1=z8, op0=ALU.subtract, op1=ALU.max))
                DR()
                for t in range(NT):
                    b.op("vector", lambda e, m=mfv[t], r=rvv[t]: e.scalar_tensor_tensor(
                        out=m, in0=r, scalar=1e30, in1=ones8[:], op0=ALU.mult, op1=ALU.min))
                for t in range(NT):
                    b.op("vector", lambda e, s=srv[t], r=rvv[t]: e.reduce_sum(s, r, axis=AX))
                DR()
                for t in range(NT):
                    b.op("vector", lambda e, c=cntv[t], m=mfv[t]: e.reduce_sum(c, m, axis=AX))
                DR()
                for t in range(NT):
                    b.op("vector", lambda e, rc=rcv[t], c=cntv[t]: e.reciprocal(rc, c))
                for t in range(NT):
                    b.op("vector", lambda e, s=srv[t]: e.tensor_scalar_add(s, s, -1.0))
                DR()
                for t in range(NT):
                    b.op("vector", lambda e, t1=t1v[t], s=srv[t], rc=rcv[t]: e.tensor_mul(t1, s, rc))
                DR()
                for t in range(NT):
                    b.op("vector", lambda e, a=tauv[t], t1=t1v[t]: e.tensor_add(a, a, t1))
            DR()
            for t in range(NT):
                b.op("vector", lambda e, d=dvv[t], z=zv[t], a=tauv[t]: e.tensor_scalar_sub(d, z, a))
            DR()
            tmich = None
            for t in range(NT):
                tmich = b.op("vector", lambda e, w=wv_[t], d=dvv[t]: e.tensor_tensor(w, d, z8, ALU.max),
                             "DVE" if t == NT - 1 else None)

            b.waits("tensor", tla)
            thm = None
            for tb in range(2):
                o_ = PS1[:, tb * 512:(tb + 1) * 512]
                for fc in range(32):
                    l_ = lat_v(fc)
                    r_ = xnf_v(fc)[:, tb * 512:(tb + 1) * 512]
                    thm = b.op("tensor", _mk_mm(o_, l_, r_, fc == 0, fc == 31),
                               "PE" if fc == 31 else None)
            b.waits("scalar", thm, tC_st0, tC_st1)
            thf = b.op("scalar", lambda e: e.copy(hf_v, PS1[:]), "ACT")

            _twT_box = [None]

            def _emit_d2b():
                b.waits("tensor", tmich, tC_done_act)
                twt = None
                for t in range(NT):
                    o_ = PSB[0:8, t * 128:(t + 1) * 128]
                    twt = b.op("tensor", _mk_tr(o_, wv_[t], c_i128[:]), "PE" if t == NT - 1 else None)
                b.waits("scalar", twt)
                _twT_box[0] = b.op("scalar", lambda e: e.copy(wTt[:], PSB[0:8, 0:1024]), "ACT")
            def _emit_wrep():
                b.waits("tensor", _twT_box[0])
                twr = None
                for tb in range(2):
                    o_ = PS2[:, tb * 512:(tb + 1) * 512]
                    r_ = wTt[0:8, tb * 512:(tb + 1) * 512]
                    twr = b.op("tensor", _mk_mm(o_, c_rep8[:], r_, True, True), "PE" if tb == 1 else None)
                b.waits("vector", twr, thf)
                return b.op("vector", lambda e: e.tensor_mul(hwf_v, PS2[:], hf_v), "DVE")
            thw = None

            wbt_r = wbt.rearrange("(fc p) c -> p fc c", p=128)
            wslot3 = lambda i: (wslot(i) if i < 2 else arb[:, 0:4096])
            tD_wslot = [tslot_pe[0], tslot_pe[1], None]
            tsg = [None] * 4
            tsg_guard3 = [None, None, None]
            srel_first2 = True
            tstt_byps = [None] * 4
            tse_last = None
            tst_prev = None
            b.waits("tensor", thf, tlb)
            tD_wslot[2] = thm
            for og in range(16):
                oc0 = og * 2
                tw = [None, None]
                for oci in range(2):
                    oc = oc0 + oci
                    slot = oc % 3
                    b.waits("gpsimd", tD_wslot[slot])
                    wdst = wslot3(slot).rearrange("p (fc c) -> p fc c", c=128)
                    wsrc = wbt_r[:, :, oc * 128:(oc + 1) * 128]
                    tw[oci] = b.dma("gpsimd", "WD%d" % slot, lambda e, d=wdst, s=wsrc: e.dma_start(out=d, in_=s))
                tsr = [None, None]
                for oci in range(2):
                    oc = oc0 + oci
                    srs = oc % 3
                    ssrc = s_scr[oc * 128:(oc + 1) * 128, :]
                    sd = srel_v(srs)
                    if srs == 2 and srel_first2:
                        b.waits("sync", thw)
                    if og == 0:
                        b.waits("sync", b.tok("DVE"))
                    if tsg_guard3[srs] is not None:
                        b.waits("sync", tsg_guard3[srs])
                    b.waits("sync", tC_st0, tC_st1)
                    tsr[oci] = b.dma("sync", "SRL%d" % srs, lambda e, d=sd, s=ssrc: e.dma_start(out=d, in_=s))
                    if srs == 2:
                        srel_first2 = False
                tmm4 = [None] * 4
                if og == 0:
                    b.waits("tensor", tw[0])
                    for fc in range(32):
                        l_ = wslot3(oc0 % 3)[:, fc * 128:(fc + 1) * 128]
                        for tb in range(2):
                            o_ = PS1[:, tb * 512:(tb + 1) * 512]
                            r_ = xnf_v(fc)[:, tb * 512:(tb + 1) * 512]
                            b.op("tensor", _mk_mm(o_, l_, r_, fc == 0, False), None)
                    _emit_d2b()
                    thw = _emit_wrep()
                for oci in range(2):
                    oc = oc0 + oci
                    slot = oc % 3
                    ps = PS1 if oci == 0 else PS2
                    if og == 0 and oci == 1:
                        b.waits("tensor", thw)
                    if not (og == 0 and oci == 0):
                        b.waits("tensor", tw[oci])
                        for fc in range(32):
                            l_ = wslot3(slot)[:, fc * 128:(fc + 1) * 128]
                            for tb in range(2):
                                o_ = ps[:, tb * 512:(tb + 1) * 512]
                                r_ = xnf_v(fc)[:, tb * 512:(tb + 1) * 512]
                                b.op("tensor", _mk_mm(o_, l_, r_, fc == 0, False), None)
                    l_ = lbt_v[:, oc * 128:(oc + 1) * 128]
                    b.waits("tensor", thw)
                    for tb in range(2):
                        o_ = ps[:, tb * 512:(tb + 1) * 512]
                        r_ = hwf_v[:, tb * 512:(tb + 1) * 512]
                        tmm4[oci * 2 + tb] = b.op("tensor", _mk_mm(o_, l_, r_, False, True),
                                                  "PE" if tb == 1 else None)
                tD_wslot[oc0 % 3] = tmm4[1]
                tD_wslot[(oc0 + 1) % 3] = tmm4[3]
                for oci in range(2):
                    oc = oc0 + oci
                    ps = PS1 if oci == 0 else PS2
                    b.waits("scalar", tmm4[oci * 2 + 1])
                    for tb in range(2):
                        gi = oci * 2 + tb
                        o_ = g_v(gi)
                        i_ = ps[:, tb * 512:(tb + 1) * 512]
                        tgv = b.op("scalar", lambda e, oo=o_, ii=i_, c=oc: e.activation(
                            oo, ii, AF.Silu, bias=c_bb[:, c:c + 1], scale=1.0), "ACT")
                        b.waits("vector", tgv, tsr[oci])
                        so_ = sg_v(gi)
                        sr_ = srel_v(oc % 3)[:, tb * 512:(tb + 1) * 512]
                        tsg[gi] = b.op("vector", lambda e, oo=so_, gg=o_, ss=sr_: e.tensor_mul(
                            oo, gg, ss), "DVE")
                tsg_guard3[oc0 % 3] = tsg[1]
                tsg_guard3[(oc0 + 1) % 3] = tsg[3]
                if tse_last is not None:
                    b.waits("sync", tse_last)
                else:
                    b.waits("sync", b.tok("DVE"), b.tok("ACT"))
                xsrc = xb.rearrange("(tc p) c -> p tc c", p=128)[:, 8:16, og * 256:(og + 1) * 256]
                xdst3 = xog_v.rearrange("p (tc c) -> p tc c", c=256)
                txl = b.dma("sync", "XG0", lambda e, d=xdst3, s=xsrc: e.dma_start(out=d, in_=s))
                tse = None
                for tb in range(2):
                    for i4l in range(4):
                        tcl = tb * 4 + i4l
                        qslot = tcl % 2
                        pq = PSB[:, qslot * 1024: qslot * 1024 + 256]
                        if tstt_byps[qslot] is not None:
                            b.waits("tensor", tstt_byps[qslot])
                        ttr = None
                        for oci in range(2):
                            gi = oci * 2 + tb
                            b.waits("tensor", tsg[gi])
                            o_ = pq[:, oci * 128:(oci + 1) * 128]
                            i_ = sg_v(gi)[:, i4l * 128:(i4l + 1) * 128]
                            ttr = b.op("tensor", _mk_tr(o_, i_, c_i128[:]),
                                       "PE" if oci == 1 else None)
                        b.waits("vector", ttr, txl, tst_prev)
                        for oci in range(2):
                            oc = oc0 + oci
                            o_ = ostg_v[:, tcl * 256 + oci * 128: tcl * 256 + (oci + 1) * 128]
                            i0 = pq[:, oci * 128:(oci + 1) * 128]
                            rz_ = rztok[:, tcl * 32 + oc: tcl * 32 + oc + 1]
                            x_ = xog_v[:, tcl * 256 + oci * 128: tcl * 256 + (oci + 1) * 128]
                            tse = b.op("vector", lambda e, oo=o_, ii=i0, rr=rz_, xx=x_:
                                       e.scalar_tensor_tensor(out=oo, in0=ii, scalar=rr, in1=xx,
                                                              op0=ALU.mult, op1=ALU.add),
                                       "DVE" if oci == 1 else None)
                        tstt_byps[qslot] = tse
                tse_last = tse
                b.waits("sync", tse_last)
                odst3 = out.rearrange("(tc p) c -> p tc c", p=128)[:, :, og * 256:(og + 1) * 256]
                osrc3 = ostg_v.rearrange("p (tc c) -> p tc c", c=256)
                tst_prev = b.dma("sync", "SO0",
                                 lambda e, d=odst3, s=osrc3: e.dma_start(out=d, in_=s))
            for qs in range(4):
                b.waits("gpsimd", b.tok("SO%d" % qs))


        for _s in list(b.ctr):
            b.waits("gpsimd", (_s, b.ctr[_s].v))

        @block.sync
        def _(e):
            for fn in b.q["sync"]:
                fn(e)

        @block.gpsimd
        def _(e):
            for fn in b.q["gpsimd"]:
                fn(e)

        @block.vector
        def _(e):
            for fn in b.q["vector"]:
                fn(e)

        @block.scalar
        def _(e):
            for fn in b.q["scalar"]:
                fn(e)

        @block.tensor
        def _(e):
            for fn in b.q["tensor"]:
                fn(e)

    return nc


def _host_consts():
    inv = 10000.0 ** (-(np.arange(0, 128, 2, dtype=np.float64) / 128.0))
    t = np.arange(SEQ, dtype=np.float64)
    ang = inv[:, None] * t[None, :]
    cos = np.cos(ang)
    sin = np.sin(ang)
    cosf = np.concatenate([cos, cos], 0).astype(np.float32)
    ssin = np.concatenate([-sin, sin], 0).astype(np.float32)
    p64 = np.zeros((128, 128), dtype=np.float32)
    for i in range(128):
        p64[(i + 64) % 128, i] = 1.0
    i128 = np.eye(128, dtype=np.float32)
    i4 = np.eye(4, dtype=np.float32)
    selm = np.zeros((128, 16), dtype=np.float32)
    for h in range(4):
        selm[:, h * 4 + h] = 1.0
    rep8 = np.zeros((8, 128), dtype=np.float32)
    for e_ in range(8):
        rep8[e_, e_ * 16:(e_ + 1) * 16] = 1.0
    return dict(cosf=cosf, ssin=ssin, p64=p64, i128=i128, i4=i4, selm=selm,
                rep8=rep8,
                onesr=np.ones((1, 128), dtype=np.float32),
                onesc=np.ones((128, 1), dtype=np.float32),
                epsv=np.full((128, 1), 1e-5, dtype=np.float32))


_NC = None


def kernel(L0_attn_x, L0_attn_wq, L0_attn_bq, L0_attn_wk, L0_attn_bk, L0_attn_wv,
           L0_attn_bv, L0_ffn_W_base, L0_ffn_b_base, L0_ffn_W_router, L0_ffn_b_router,
           L0_ffn_lora_A, L0_ffn_lora_B):
    global _NC
    if _NC is None:
        _NC = build_nc()
    nc = _NC

    x = np.ascontiguousarray(np.asarray(L0_attn_x, dtype=np.float32))
    C = _host_consts()
    shared = {
        "wq": np.ascontiguousarray(np.asarray(L0_attn_wq, np.float32)),
        "wk": np.ascontiguousarray(np.asarray(L0_attn_wk, np.float32)),
        "wv": np.ascontiguousarray(np.asarray(L0_attn_wv, np.float32)),
        "wrt": np.ascontiguousarray(np.asarray(L0_ffn_W_router, np.float32).T),
        "wbt": np.ascontiguousarray(np.asarray(L0_ffn_W_base, np.float32).T),
        "la": np.ascontiguousarray(
            np.asarray(L0_ffn_lora_A, np.float32).transpose(1, 0, 2).reshape(DM, 128)),
        "lb": np.ascontiguousarray(np.asarray(L0_ffn_lora_B, np.float32).reshape(128, DM)),
        "bq2": np.ascontiguousarray(np.asarray(L0_attn_bq, np.float32).reshape(32, 128).T),
        "bk2": np.ascontiguousarray(np.asarray(L0_attn_bk, np.float32).reshape(8, 128).T),
        "bb2": np.ascontiguousarray(np.asarray(L0_ffn_b_base, np.float32).reshape(32, 128).T),
        "bvr": np.ascontiguousarray(np.asarray(L0_attn_bv, np.float32).reshape(1, 1024)),
        "bv2": np.ascontiguousarray(np.asarray(L0_attn_bv, np.float32).reshape(8, 128).T),
        "brr": np.ascontiguousarray(np.asarray(L0_ffn_b_router, np.float32).reshape(1, 8)),
        **C,
    }
    in_maps = []
    for c in range(8):
        bidx = c // 2
        half = c % 2
        order = np.concatenate([np.arange((1 - half) * OWN, (2 - half) * OWN),
                                np.arange(half * OWN, (half + 1) * OWN)])
        m = dict(shared)
        m["xb"] = np.ascontiguousarray(x[bidx][order])
        m["cosf"] = np.ascontiguousarray(C["cosf"][:, order])
        m["ssin"] = np.ascontiguousarray(C["ssin"][:, order])
        in_maps.append(m)
    res = run_bass_kernel_spmd(nc, in_maps, core_ids=list(range(8)))
    outp = np.empty((4, SEQ, DM), dtype=np.float32)
    for c in range(8):
        bidx = c // 2
        half = c % 2
        outp[bidx, half * OWN:(half + 1) * OWN, :] = res.results[c]["out"]
    return outp
